# version 27
# baseline (speedup 1.0000x reference)
import os
import sys
import numpy as np

for _p in ("/opt/trn_rl_repo",):
    if _p not in sys.path and os.path.isdir(_p):
        sys.path.insert(0, _p)

import ml_dtypes
import concourse.bass as bass
import concourse.mybir as mybir
from concourse.bass_utils import run_bass_kernel_spmd

F32 = mybir.dt.float32
BF16 = mybir.dt.bfloat16
AF = mybir.ActivationFunctionType
ALU = mybir.AluOpType
AX = mybir.AxisListType
BFNP = ml_dtypes.bfloat16

NK = 64
NTILE = 33
NST = 17
NHEAD = 8
NPT = 40
NHIDX = NST * NHEAD
SDA_H = NK + 3
SEV_H = 10

SEMS = ["smm", "sex", "sda", "syb", "sye", "sev", "strm", "swv",
        "svo", "safo", "sto", "skw", "sxm", "sst", "sph",
        "lsm", "lw0", "lw1", "lw2", "lw3", "lxf0", "lxf1", "lxf2", "lxf3", "lxf4", "lxf5",
        "lfw", "lxm0", "lxm1", "lxm2", "lxm3", "lxm4", "lxm5",
        "skt0", "skt1", "swt0", "swt1", "sqt0", "sqt1", "sxq0", "sxq1"]

NOINC = "__noinc__"

CORD = [4, 5, 6, 7, 0, 1, 2, 3]
CPOS = {c: i for i, c in enumerate(CORD)}


def st_tiles(st):
    return [2 * st, 2 * st + 1] if st < 16 else [32]


def st_qw(st):
    return 256 if st < 16 else 128


class Rec(dict):
    def __missing__(self, key):
        return None


class Eng:

    def __init__(self, name, counts, fb, record=False):
        self.name = name
        self.counts = counts
        self.fb = fb
        self.waited = {}
        self.prev = None
        self.ops = [] if record else None

    def wait(self, sem, v):
        if v is None or v <= 0:
            return
        if self.waited.get(sem, 0) >= v:
            return
        self.waited[sem] = v
        if self.ops is not None:
            self.ops.append(("w", sem, v))

    def waitp(self, pv):
        if pv is not None:
            self.wait(*pv)

    def call(self, meth, *args, inc=NOINC, dinc=1, chain=False, **kwargs):
        if chain:
            self.waitp(self.prev)
        if inc is NOINC:
            sem = None
        else:
            sem = inc if inc is not None else self.fb
            self.counts[sem] += dinc
            self.prev = (sem, self.counts[sem])
        if self.ops is not None:
            self.ops.append(("c", meth, args, kwargs, sem, dinc))
        return self.prev


def replay(eng, sems, ops):
    for op in ops:
        if op[0] == "w":
            eng.wait_ge(sems[op[1]], op[2])
        else:
            _, meth, args, kwargs, sem, dinc = op
            ins = getattr(eng, meth)(*args, **kwargs)
            if sem is not None:
                ins.then_inc(sems[sem], dinc)



def emit_tensor(T, M, rec):
    banks = M["banks"]
    identb, identf = M["identb"], M["identf"]

    def y_ps(c, qw):
        return banks[2 + (c % 4)][:, (c // 4) * 256:(c // 4) * 256 + qw]

    T.wait("lsm", 16 * 6)
    for job in range(len(M["wjobs"])):
        T.waitp(rec[("wconv", job)])
        for j in range(8):
            if job >= 1:
                T.waitp(rec[("weg", job - 1, j % 2)])
            T.call("transpose",
                   banks[j % 2][:].bitcast(BF16)[:, (j // 2) * 128:(j // 2 + 1) * 128],
                   M["wldb"][job % 4][:, j * 128:(j + 1) * 128], identb[:],
                   inc="sto")
    def p1_tr(idx):
        is_q = idx >= NK
        srct = (M["x_ln"][:, idx * 1024:(idx + 1) * 1024] if not is_q
                else M["qlnb"][idx % 4][:])
        T.waitp(rec[("lncopy", idx)])
        for j in range(8):
            if idx >= 1:
                T.waitp(rec[("p1eg", idx - 1, j % 2)])
            T.call("transpose",
                   banks[2 + (j % 2)][:].bitcast(BF16)[:, (j // 2) * 128:(j // 2 + 1) * 128],
                   srct[:, j * 128:(j + 1) * 128], identb[:], inc="sto")

    def p1_mm(idx):
        is_q = idx >= NK
        T.waitp(rec[("p1eg", idx, 1)])
        if idx >= 4:
            T.waitp(rec[("p1stage", idx - 4)])
        wT = M["wqT_p1"] if is_q else M["wkT_p1"]
        pb = banks[4 + idx % 4]
        for p in range(4):
            for j in range(8):
                last = (p == 3 and j == 7)
                T.call("matmul", pb[:, p * 128:(p + 1) * 128],
                       wT[:, j * 512 + p * 128:j * 512 + (p + 1) * 128],
                       M["xlt"][idx % 4][:, j * 128:(j + 1) * 128],
                       start=(p == 0 and j == 0), stop=last,
                       skip_group_check=True, inc=("sto" if last else NOINC))
        rec[("p1mm", idx)] = T.prev

    p1_tr(0)
    for idx in range(NK + NTILE):
        if idx + 1 < NK + NTILE:
            p1_tr(idx + 1)
        p1_mm(idx)

    T.waitp(T.prev)
    T.call("nop", inc="sph")
    T.wait("sph", 5)

    def spair(hidx, k):
        st, h = hidx // NHEAD, hidx % NHEAD
        qw = st_qw(st)
        ghq = hidx * 4 + k // 16
        T.wait(f"skt{ghq % 2}", 32 * (ghq // 2 + 1))
        T.wait(f"sqt{st % 2}", 256 * (st // 2 + 1))
        T.wait("sex", hidx * NK + k)
        kb = M["kt_buf"][ghq % 2]
        qb = M["qt_buf"][st % 2]
        km = ((k % 16) // 2) * 128
        T.call("matmul", banks[0][:, 0:qw], kb[0:64, km:km + 128],
               qb[0:64, h * 256:h * 256 + qw], start=True, stop=True)
        T.call("matmul", banks[1][:, 0:qw], kb[64:128, km:km + 128],
               qb[64:128, h * 256:h * 256 + qw], start=True, stop=True,
               inc="smm")
        rec[("ktfree", ghq)] = ("smm", T.counts["smm"])
        rec[("qtfree", st)] = ("smm", T.counts["smm"])

    NPAIR = NHIDX * 32

    def spair_g(q):
        if q < NPAIR:
            spair(q // 32, (q % 32) * 2)

    LEAD = 4
    for q in range(8 + LEAD):
        spair_g(q)
    for hidx in range(NHIDX):
        st, h = hidx // NHEAD, hidx % NHEAD
        qw = st_qw(st)
        tiles = st_tiles(st)
        for kg in range(4):
            for ci, c in enumerate(CORD):
                T.wait("sex", hidx * NK + kg * 16 + 16)
                if kg == 3 and ci >= 4:
                    T.wait("sev", hidx * SEV_H + (ci - 4) + 1)
                if kg == 0 and hidx > 0:
                    T.wait("sev", (hidx - 1) * SEV_H + ci + 1
                           if False else (hidx - 1) * SEV_H + CPOS[c] + 1)
                    if ci < 2:
                        T.wait("sda", hidx * SDA_H)
                for kk in range(16):
                    k = kg * 16 + kk
                    gch = hidx * NK + k
                    last = (kk == 15)
                    inc = NOINC
                    if last and kg == 3:
                        inc = "sye"
                    elif last and ci == 7:
                        inc = "syb"
                    T.call("matmul", y_ps(c, qw),
                           M["x_ln"][:, k * 1024 + c * 128:k * 1024 + (c + 1) * 128],
                           M["pt"][gch % NPT][:, 0:qw],
                           start=(kg == 0 and kk == 0 and ci < 4),
                           stop=(kg == 3 and kk == 15),
                           skip_group_check=True, inc=inc)
                bg = (hidx * 4 + kg) * 8 + ci
                spair_g(bg + 8 + LEAD)
        for i in range(2 if qw == 256 else 1):
            T.wait("sev", hidx * SEV_H + 5 + i)
            T.wait("sda", hidx * SDA_H + NK)
            T.call("transpose", banks[2 + i][:, 0:128],
                   M["dacc"][:, i * 128:(i + 1) * 128], identf[:], inc="strm",
                   dinc=(1 if qw == 256 else 2))
        T.wait("swt0", (hidx + 1) * 16)
        T.wait("swt1", (hidx + 1) * 16)
        for ti in range(len(tiles)):
            for c in range(8):
                T.wait("sev", hidx * SEV_H + CPOS[c] + 1)
                if ti == 1 and c == 0:
                    T.wait("sev", hidx * SEV_H + 9)
                half, cc = c // 4, c % 4
                T.call("matmul", banks[6][:, 0:512],
                       M["yt"][c][:, ti * 128:ti * 128 + 128],
                       M["wvt_buf"][half][:, cc * 1024:cc * 1024 + 512],
                       start=(c == 0), stop=(c == 7), skip_group_check=True)
                last = (c == 7)
                T.call("matmul", banks[7][:, 0:512],
                       M["yt"][c][:, ti * 128:ti * 128 + 128],
                       M["wvt_buf"][half][:, cc * 1024 + 512:(cc + 1) * 1024],
                       start=(c == 0), stop=(c == 7), skip_group_check=True,
                       inc=("swv" if last else NOINC),
                       dinc=(2 if (last and len(tiles) == 1) else 1))

    T.waitp(T.prev)
    T.call("nop", inc="sph")
    T.wait("sph", 10)

    for ti in range(NTILE):
        T.waitp(rec[("ucopy", ti)])
        for j in range(8):
            if ti >= 1:
                T.waitp(rec[("ueg", ti - 1, j % 2)])
            T.call("transpose",
                   banks[j % 2][:].bitcast(BF16)[:, (j // 2) * 128:(j // 2 + 1) * 128],
                   M["ub"][ti % 4][:, j * 128:(j + 1) * 128], identb[:], inc="sto")

    T.waitp(rec["p3a_v_done"])
    T.wait("lfw", 16 * 3)
    for g in range(16):
        T.waitp(rec[("mix", g)])
        for ti in range(2):
            T.waitp(rec[("o2", g - 1, ti)])
            for j in range(8):
                T.call("matmul", banks[4 + 2 * ti][:, 0:512],
                       M["xrT"][g % 2][:, j * 256 + ti * 128:j * 256 + ti * 128 + 128],
                       M["wrT_sb"][:, j * 1024:j * 1024 + 512],
                       start=(j == 0), stop=False, skip_group_check=True)
                T.call("matmul", banks[5 + 2 * ti][:, 0:512],
                       M["xrT"][g % 2][:, j * 256 + ti * 128:j * 256 + ti * 128 + 128],
                       M["wrT_sb"][:, j * 1024 + 512:(j + 1) * 1024],
                       start=(j == 0), stop=(j == 7), skip_group_check=True,
                       inc=("strm" if j == 7 else NOINC))
            rec[("m2", g, ti)] = T.prev
        T.waitp(rec[("relu", g - 1)])
        for j in range(8):
            for cj in range(8):
                T.call("matmul",
                       banks[j % 4][:, (j // 4) * 256:(j // 4) * 256 + 256],
                       M["wkT_sb"][:, cj * 1024 + j * 128:cj * 1024 + (j + 1) * 128],
                       M["xkT"][g % 2][:, cj * 256:(cj + 1) * 256],
                       start=(cj == 0 and j < 4), stop=(cj == 7),
                       skip_group_check=True,
                       inc=("strm" if cj == 7 else NOINC))
            rec[("m1", g, j)] = T.prev
        T.waitp(rec[("kact", g)])
        for ti in range(2):
            T.waitp(rec[("sigexp", g, ti)])
            for j in range(8):
                T.call("matmul", banks[4 + 2 * ti][:, 0:512],
                       M["kactT"][:, j * 256 + ti * 128:j * 256 + ti * 128 + 128],
                       M["wvT_sb"][:, j * 1024:j * 1024 + 512],
                       start=(j == 0), stop=False, skip_group_check=True)
                T.call("matmul", banks[5 + 2 * ti][:, 0:512],
                       M["kactT"][:, j * 256 + ti * 128:j * 256 + ti * 128 + 128],
                       M["wvT_sb"][:, j * 1024 + 512:(j + 1) * 1024],
                       start=(j == 0), stop=(j == 7), skip_group_check=True,
                       inc=("strm" if j == 7 else NOINC))
            rec[("m3", g, ti)] = T.prev


_LN_LAST = {}


def emit_ln(V, A, M, src_f32, dst, rec_key=None, rec=None, par=0):
    half = src_f32.shape[-1] // 2
    st6 = M["stats6"][:, par * 12:par * 12 + 12]
    st2 = M["stats2"][:, par * 2:par * 2 + 2]
    nmr = M["nmr"][:, par:par + 1]
    rstd = M["rstd"][:, par:par + 1]
    war = _LN_LAST.get((id(A), par))
    V.call("bn_stats", st6[:, 0:6], src_f32[:, 0:half], chain=True, inc=None)
    V.call("bn_stats", st6[:, 6:12], src_f32[:, half:2 * half],
           chain=True, inc=None)
    if war is not None:
        V.wait("safo", war)
    V.call("bn_aggr", st2, st6, chain=True, inc=None)
    A.waitp(V.prev)
    A.call("activation", rstd, st2[:, 1:2], AF.Ln,
           bias=M["consts"][:, 0:1], chain=True, inc=None)
    A.call("activation", rstd, rstd, AF.Exp,
           scale=M["consts"][:, 1:2], chain=True, inc=None)
    V.waitp(A.prev)
    V.call("tensor_scalar", nmr, st2[:, 0:1], -1.0, rstd,
           ALU.mult, ALU.mult, chain=True, inc=None)
    A.waitp(V.prev)
    A.call("activation", dst, src_f32, AF.Identity, bias=nmr,
           scale=rstd, chain=True, inc=None)
    _LN_LAST[(id(A), par)] = A.counts["safo"]
    if rec_key is not None:
        rec[rec_key] = A.prev


def ln_s1(V, A, M, rec, key, src, par):
    st6 = M["stats6"][:, par * 12:par * 12 + 12]
    war = rec[("lnae", key[0], key[1] - 4)] if key[1] >= 4 else None
    half = src.shape[-1] // 2
    V.call("bn_stats", st6[:, 0:6], src[:, 0:half], chain=True, inc=None)
    V.waitp(war)
    V.call("bn_stats", st6[:, 6:12], src[:, half:2 * half], chain=True, inc=None)
    V.call("bn_aggr", M["stats2"][:, par * 2:par * 2 + 2], st6, chain=True,
           inc=None)
    rec[("lnagg", key[0], key[1])] = V.prev


def ln_s2(V, A, M, rec, key, par):
    A.waitp(rec[("lnagg", key[0], key[1])])
    A.call("activation", M["rstd"][:, par:par + 1],
           M["stats2"][:, par * 2 + 1:par * 2 + 2], AF.Ln,
           bias=M["consts"][:, 0:1], chain=True, inc=None)
    A.call("activation", M["rstd"][:, par:par + 1], M["rstd"][:, par:par + 1],
           AF.Exp, scale=M["consts"][:, 1:2], chain=True, inc=None)
    rec[("lnae", key[0], key[1])] = A.prev


def ln_s3(V, A, M, rec, key, par):
    V.waitp(rec[("lnae", key[0], key[1])])
    if key[1] >= 4:
        V.waitp(rec[("lncp", key[0], key[1] - 4)])
    V.call("tensor_scalar", M["nmr"][:, par:par + 1],
           M["stats2"][:, par * 2:par * 2 + 1], -1.0, M["rstd"][:, par:par + 1],
           ALU.mult, ALU.mult, chain=True, inc=None)
    rec[("lnnm", key[0], key[1])] = V.prev


def ln_s4(V, A, M, rec, key, src, dst, par):
    A.waitp(rec[("lnnm", key[0], key[1])])
    A.call("activation", dst, src, AF.Identity, bias=M["nmr"][:, par:par + 1],
           scale=M["rstd"][:, par:par + 1], chain=True, inc=None)
    rec[("lncp", key[0], key[1])] = A.prev


def emit_scalar_vector(A, V, M, rec):
    banks = M["banks"]

    A.wait("lsm", 16 * 6)
    V.wait("lsm", 16 * 6)

    def p0_conv(job):
        V.wait(f"lw{job % 4}", 16 * (job // 4 + 1))
        if job >= 4:
            V.waitp(rec[("wtr", job - 4, 7)])
        V.call("tensor_copy", M["wldb"][job % 4][:], M["wldf"][job % 4][:],
               chain=True, inc=None)
        rec[("wconv", job)] = V.prev

    for j0 in range(min(3, len(M["wjobs"]))):
        p0_conv(j0)
    for job in range(len(M["wjobs"])):
        if job + 3 < len(M["wjobs"]):
            p0_conv(job + 3)
        if M["wjob_prev_write"][job] is not None:
            V.wait("skw", 16 * M["wjob_prev_write"][job])
        for par in range(2):
            V.waitp(rec[("wtr", job, 6 + par)])
            V.call("tensor_copy", M["wjob_destg"][job][par],
                   M["banks"][par][:].bitcast(BF16)[:, 0:512].rearrange(
                       "p (c b) -> p c b", b=128),
                   chain=True, inc=None)
            rec[("weg", job, par)] = V.prev

    NP1 = NK + NTILE

    def p1_src(i):
        return M["xf"][(i + 4) % 6][:]

    def p1_dst(i):
        return (M["x_ln"][:, i * 1024:(i + 1) * 1024] if i < NK
                else M["qlnb"][i % 4][:])

    def p1_evac(idx):
        for par in range(2):
            V.waitp(rec[("p1tr", idx, 6 + par)])
            V.call("tensor_copy",
                   M["xlt"][idx % 4][:].rearrange("p (j2 two b) -> p j2 two b",
                                                  two=2, b=128)[:, :, par, :],
                   M["banks"][2 + par][:].bitcast(BF16)[:, 0:512].rearrange(
                       "p (c b) -> p c b", b=128),
                   chain=True, inc=None)
            rec[("p1eg", idx, par)] = V.prev

    def p1_stage(idx):
        V.waitp(rec[("p1mm", idx)])
        pbk = banks[4 + idx % 4]
        if idx < NK:
            V.wait("skw", 16 * (M["NKW0"] + 4 * (idx // 8)))
            g = idx % 8
            dst3 = M["ktstage"][:].rearrange("p (a f) -> p a f", a=4)[:, :, g * 128:(g + 1) * 128]
        else:
            qi = idx - NK
            V.wait("skw", 16 * (M["NKW0"] + 32 + 4 * (qi // 2)))
            dst3 = M["qtstage"][:].rearrange("p (a f) -> p a f", a=4)[:, :, (qi % 2) * 128:(qi % 2) * 128 + 128]
        src3 = pbk[:].rearrange("p (a f) -> p a f", a=4)
        V.call("tensor_copy", dst3, src3, chain=True, inc=None)
        rec[("p1stage", idx)] = V.prev

    for i in range(NP1 + 5):
        if i < NP1:
            V.wait(f"lxf{(i + 4) % 6}", 16 * (i // 6 + 1))
            ln_s1(V, A, M, rec, ("p1", i), p1_src(i), i % 4)
        if 0 <= i - 1 < NP1:
            ln_s2(V, A, M, rec, ("p1", i - 1), (i - 1) % 4)
        if 0 <= i - 2 < NP1:
            ln_s3(V, A, M, rec, ("p1", i - 2), (i - 2) % 4)
        if 0 <= i - 3 < NP1:
            j = i - 3
            if j >= NK and j - 4 >= NK:
                A.waitp(rec[("p1tr", j - 4, 7)])
            ln_s4(V, A, M, rec, ("p1", j), p1_src(j), p1_dst(j), j % 4)
            rec[("lncopy", j)] = rec[("lncp", "p1", j)]
        if 0 <= i - 4 < NP1:
            p1_evac(i - 4)
        if 0 <= i - 5 < NP1:
            p1_stage(i - 5)

    A.waitp(A.prev)
    A.call("nop", inc="sph")
    A.wait("sph", 5)
    V.waitp(V.prev)
    V.call("nop", inc="sph")
    V.wait("sph", 5)

    def epilogue(st):
        tiles = st_tiles(st)
        for ii, tile in enumerate(tiles):
            V.wait(f"sxq{tile % 2}", 16 * (tile // 2 + 1))
            if ii == 1:
                A.waitp(rec[("xmid", tiles[0])])
            emit_ln(V, A, M, M["out_sb"][ii][:], M["tmpf"][:],
                    ("epicopy", st, ii), rec, par=ii)
            V.waitp(A.prev)
            V.call("tensor_add", M["xq_buf"][tile % 2][:],
                   M["xq_buf"][tile % 2][:], M["tmpf"][:], chain=True, inc=None)
            rec[("xmid", tile)] = V.prev

    for hidx in range(NHIDX):
        st, h = hidx // NHEAD, hidx % NHEAD
        qw = st_qw(st)
        tiles = st_tiles(st)
        for k in range(NK):
            gch = hidx * NK + k
            A.wait("smm", gch // 2 + 1)
            jj = gch - NPT
            if jj >= 0:
                jh, jk = jj // NK, jj % NK
                kgj = jk // 16
                if kgj == 3:
                    A.wait("sye", jh * 8 + 8)
                else:
                    A.wait("syb", jh * 3 + kgj + 1)
                A.wait("sda", jh * SDA_H + min((kgj + 1) * 16, NK))
            A.call("activation", M["pt"][gch % NPT][:, 0:qw],
                   banks[k % 2][:, 0:qw], AF.Exp, inc="sex")
            if h == 0 and st >= 1 and k == 35:
                pass
        for k in range(NK):
            gch = hidx * NK + k
            V.wait("sex", hidx * NK + k + 1)
            if k == 0:
                V.wait("strm", 2 * hidx)
                V.call("tensor_copy", M["dacc"][:, 0:qw], M["pt"][gch % NPT][:, 0:qw],
                       chain=True, inc="sda")
            else:
                V.call("tensor_add", M["dacc"][:, 0:qw], M["dacc"][:, 0:qw],
                       M["pt"][gch % NPT][:, 0:qw], chain=True, inc="sda")
        if h == 0 and st >= 1:
            epilogue(st - 1)
        for ci, c in enumerate(CORD):
            V.wait("sye", hidx * 8 + ci + 1)
            if hidx > 0 and ci == 0:
                V.wait("swv", hidx * 2)
            V.call("tensor_copy", M["yt"][c][:, 0:qw],
                   banks[2 + (c % 4)][:, (c // 4) * 256:(c // 4) * 256 + qw],
                   chain=True, inc="sev")
        for i in range(2 if qw == 256 else 1):
            V.wait("strm", hidx * 2 + (i + 1 if qw == 256 else 2))
            V.call("reduce_sum", M["recip"][:, i:i + 1],
                   banks[2 + i][:, 0:128], axis=AX.X, chain=True, inc="sda",
                   dinc=(1 if qw == 256 else 2))
        V.call("reciprocal", M["recip"][:, 0:2], M["recip"][:, 0:2],
               chain=True, inc="sda")
        for ti in range(len(tiles)):
            V.wait("swv", hidx * 2 + (ti + 1 if len(tiles) == 2 else 2))
            if h == 0:
                V.waitp(rec[("epicopy", st - 1, ti)])
            ob = M["out_sb"][ti]
            rc = M["recip"][:, ti:ti + 1]
            d2 = 2 if (len(tiles) == 1) else 1
            if h == 0:
                V.call("tensor_scalar_mul", ob[:, 0:512], banks[6][:], rc,
                       chain=True, inc=None)
                V.call("tensor_scalar_mul", ob[:, 512:1024], banks[7][:], rc,
                       chain=True, inc="sev", dinc=d2)
            else:
                V.call("scalar_tensor_tensor", ob[:, 0:512], banks[6][:], rc,
                       ob[:, 0:512], ALU.mult, ALU.add, chain=True, inc=None)
                V.call("scalar_tensor_tensor", ob[:, 512:1024], banks[7][:], rc,
                       ob[:, 512:1024], ALU.mult, ALU.add, chain=True, inc="sev",
                       dinc=d2)

    epilogue(NST - 1)
    A.waitp(A.prev)
    A.call("nop", inc="sph")
    A.wait("sph", 10)
    V.waitp(V.prev)
    V.call("nop", inc="sph")
    V.wait("sph", 10)


    def p3a_evac(ti):
        for par in range(2):
            V.waitp(rec[("utr", ti, 6 + par)])
            V.call("tensor_copy",
                   M["uTall"][:].rearrange("p (j2 two q) -> p j2 two q",
                                           two=2, q=NTILE * 128)[:, :, par,
                                                                 ti * 128:(ti + 1) * 128],
                   M["banks"][par][:].bitcast(BF16)[:, 0:512].rearrange(
                       "p (c b) -> p c b", b=128),
                   chain=True, inc=None)
            rec[("ueg", ti, par)] = V.prev
        if ti == 0:
            for j in range(8):
                V.call("tensor_scalar_mul", M["uT"][j][:, 127:128],
                       M["uT"][j][:, 127:128], M["mask_sb"][:, 0:1],
                       chain=True, inc=None)

    for i in range(NTILE + 6):
        if i < NTILE:
            V.wait(f"lxm{(i + 4) % 6}", 16 * (i // 6 + 1))
            ln_s1(V, A, M, rec, ("ua", i), M["xmb"][(i + 4) % 6][:], i % 4)
        if 0 <= i - 1 < NTILE:
            ln_s2(V, A, M, rec, ("ua", i - 1), (i - 1) % 4)
        if 0 <= i - 2 < NTILE:
            ln_s3(V, A, M, rec, ("ua", i - 2), (i - 2) % 4)
        if 0 <= i - 3 < NTILE:
            j = i - 3
            if j >= 4:
                A.waitp(rec[("utr", j - 4, 7)])
            ln_s4(V, A, M, rec, ("ua", j), M["xmb"][(j + 4) % 6][:],
                  M["ub"][j % 4][:], j % 4)
            rec[("ucopy", j)] = rec[("lncp", "ua", j)]
        if 0 <= i - 6 < NTILE:
            p3a_evac(i - 6)
    rec["p3a_v_done"] = V.prev

    def mixes(g):
        V.waitp(rec[("m2", g - 2, 1)])
        V.waitp(rec[("m1", g - 2, 7)])
        for j in range(8):
            cur = 128 + g * 256
            V.call("tensor_sub", M["d1"][:], M["uT"][j][:, cur:cur + 256],
                   M["uT"][j][:, cur - 1:cur + 255], chain=True, inc=None)
            V.call("scalar_tensor_tensor", M["xrT"][g % 2][:, j * 256:(j + 1) * 256],
                   M["d1"][:], M["mur_sb"][:, j:j + 1],
                   M["uT"][j][:, cur - 1:cur + 255], ALU.mult, ALU.add,
                   chain=True, inc=None)
            V.call("scalar_tensor_tensor", M["xkT"][g % 2][:, j * 256:(j + 1) * 256],
                   M["d1"][:], M["muk_sb"][:, j:j + 1],
                   M["uT"][j][:, cur - 1:cur + 255], ALU.mult, ALU.add,
                   chain=True, inc=None)
        rec[("mix", g)] = V.prev

    mixes(0)
    mixes(1)
    NG = 16
    for g in range(NG + 1):
        ge = g - 1
        if g < NG:
            for ti in range(2):
                A.waitp(rec[("m2", g, ti)])
                A.waitp(rec[("o2", g - 1, ti)])
                A.call("activation", M["rgate"][ti][:, 0:512], banks[4 + 2 * ti][:],
                       AF.Tanh, scale=M["consts"][:, 3:4], chain=True, inc=None)
                A.call("activation", M["rgate"][ti][:, 512:1024], banks[5 + 2 * ti][:],
                       AF.Tanh, scale=M["consts"][:, 3:4], chain=True, inc=None)
                rec[("sigexp", g, ti)] = A.prev
        if ge >= 0:
            for ti in range(2):
                st6 = M["stats6"][:, ti * 12:ti * 12 + 12]
                src = M["o2"][ti][:]
                V.call("bn_stats", st6[:, 0:6], src[:, 0:512], chain=True, inc=None)
                V.call("bn_stats", st6[:, 6:12], src[:, 512:1024],
                       chain=True, inc=None)
                if ge == 0:
                    V.waitp(rec[("lncp", "ua", 32 if ti == 0 else 29)])
                V.waitp(rec[("eplnA", ge - 1, ti)])
                V.call("bn_aggr", M["stats2"][:, ti * 2:ti * 2 + 2], st6,
                       chain=True, inc=None)
                rec[("epagg", ge, ti)] = V.prev
        if g < NG:
            for ti in range(2):
                V.waitp(rec[("sigexp", g, ti)])
                V.call("tensor_scalar_add", M["rgate"][ti][:], M["rgate"][ti][:],
                       1.0, chain=True, inc=None)
                rec[("rgate", g, ti)] = V.prev
            for j in range(8):
                A.waitp(rec[("m1", g, j + 4 if j < 4 else j)])
                if j == 0:
                    A.waitp(rec[("kact", g - 1)])
                A.call("activation", M["krelu"][:, j * 256:(j + 1) * 256],
                       banks[j % 4][:, (j // 4) * 256:(j // 4) * 256 + 256],
                       AF.Relu, chain=True, inc=None)
                rec[("relu1", g, j)] = A.prev
            rec[("relu", g)] = A.prev
            for j in range(8):
                V.waitp(rec[("relu1", g, j)])
                if j == 0:
                    V.waitp(rec[("m3", g - 1, 1)])
                V.call("tensor_mul", M["kactT"][:, j * 256:(j + 1) * 256],
                       M["krelu"][:, j * 256:(j + 1) * 256],
                       M["krelu"][:, j * 256:(j + 1) * 256], chain=True, inc=None)
            rec[("kact", g)] = V.prev
        if ge >= 0:
            for ti in range(2):
                A.waitp(rec[("epagg", ge, ti)])
                if ge == 0:
                    A.waitp(rec[("lnnm", "ua", 32 if ti == 0 else 29)])
                A.waitp(rec[("epnmr", ge - 1, ti)])
                A.call("activation", M["rstd"][:, ti:ti + 1],
                       M["stats2"][:, ti * 2 + 1:ti * 2 + 2], AF.Ln,
                       bias=M["consts"][:, 0:1], chain=True, inc=None)
                A.call("activation", M["rstd"][:, ti:ti + 1],
                       M["rstd"][:, ti:ti + 1], AF.Exp,
                       scale=M["consts"][:, 1:2], chain=True, inc=None)
                rec[("eplnA", ge, ti)] = A.prev
            for ti in range(2):
                V.waitp(rec[("eplnA", ge, ti)])
                V.waitp(rec[("ln3copy", ge - 1, ti)])
                V.call("tensor_scalar", M["nmr"][:, ti:ti + 1],
                       M["stats2"][:, ti * 2:ti * 2 + 1], -1.0,
                       M["rstd"][:, ti:ti + 1], ALU.mult, ALU.mult,
                       chain=True, inc=None)
                rec[("epnmr", ge, ti)] = V.prev
            for ti in range(2):
                A.waitp(rec[("epnmr", ge, ti)])
                A.wait("sst", 16 * (2 * (ge - 1) + ti + 1))
                A.call("activation", M["ystage"][ti][:], M["o2"][ti][:],
                       AF.Identity, bias=M["nmr"][:, ti:ti + 1],
                       scale=M["rstd"][:, ti:ti + 1], chain=True, inc=None)
                rec[("ln3copy", ge, ti)] = A.prev
                nld = NTILE + 2 * ge + ti
                V.wait(f"lxm{(nld + 4) % 6}", 16 * (nld // 6 + 1))
                V.waitp(rec[("ln3copy", ge, ti)])
                V.call("tensor_add", M["ystage"][ti][:], M["ystage"][ti][:],
                       M["xmb"][(nld + 4) % 6][:], chain=True, inc=None)
                rec[("yready", 2 * ge + ti)] = V.prev
        if g < NG:
            for ti in range(2):
                V.waitp(rec[("m3", g, ti)])
                V.waitp(rec[("rgate", g, ti)])
                V.waitp(rec[("ln3copy", g - 1, ti)])
                V.call("scalar_tensor_tensor", M["o2"][ti][:, 0:512],
                       banks[4 + 2 * ti][:], 0.5, M["rgate"][ti][:, 0:512],
                       ALU.mult, ALU.mult, chain=True, inc=None)
                V.call("scalar_tensor_tensor", M["o2"][ti][:, 512:1024],
                       banks[5 + 2 * ti][:], 0.5, M["rgate"][ti][:, 512:1024],
                       ALU.mult, ALU.mult, chain=True, inc=None)
                rec[("o2", g, ti)] = V.prev
            if g + 2 < NG:
                mixes(g + 2)


def emit_dma(SY, G, M, rec):
    def xf_issue(idx):
        src = (M["xb_d"][idx * 128:(idx + 1) * 128, :] if idx < NK
               else M["xq_d"][(idx - NK) * 128:(idx - NK + 1) * 128, :])
        s = (idx + 4) % 6
        SY.call("dma_start", out=M["xf"][s][:], in_=src,
                inc=f"lxf{s}", dinc=16)

    for dst, src in M["small_loads"]:
        SY.call("dma_start", out=dst, in_=src, inc="lsm", dinc=16)
    for idx in range(2):
        xf_issue(idx)
    for job in range(len(M["wjobs"])):
        if job >= 4:
            SY.waitp(rec[("wconv", job - 4)])
        SY.call("dma_start", out=M["wldf"][job % 4][:], in_=M["wjob_src"][job],
                inc=f"lw{job % 4}", dinc=16)
    for idx in range(2, NK + NTILE):
        if idx < 6:
            SY.waitp(rec[("wconv", len(M["wjobs"]) - 1)])
        else:
            SY.waitp(rec[("lncopy", idx - 6)])
        xf_issue(idx)

    for w in M["wmat_writes"]:
        G.waitp(rec[("weg", w["last_job"], 1)])
        G.call("dma_start", out=w["dst"], in_=M["wstage"][:], inc="skw", dinc=16)
        G.wait("skw", G.counts["skw"])
    for gi in range(8):
        G.waitp(rec[("p1stage", gi * 8 + 7)])
        for p in range(4):
            G.call("dma_start",
                   out=M["kt_s"][p][:, gi * 1024:(gi + 1) * 1024],
                   in_=M["ktstage"][:].rearrange("p (a f) -> p a f", a=4)[:, p],
                   inc="skw", dinc=16)
        G.wait("skw", G.counts["skw"])
    for qg in range(17):
        qts = [2 * qg, 2 * qg + 1] if qg < 16 else [32]
        G.waitp(rec[("p1stage", NK + qts[-1])])
        w = len(qts) * 128
        for p in range(4):
            G.call("dma_start",
                   out=M["qt_s"][p][:, qts[0] * 128:qts[0] * 128 + w],
                   in_=M["qtstage"][:].rearrange("p (a f) -> p a f", a=4)[:, p][:, 0:w],
                   inc="skw", dinc=16)
        G.wait("skw", G.counts["skw"])

    SY.call("nop", inc="sph")
    SY.wait("sph", 5)
    G.wait("skw", 16 * (M["NKW0"] + 32 + 68))
    G.call("nop", inc="sph")
    G.wait("sph", 5)

    def load_qt(st):
        SY.waitp(rec[("qtfree", st - 2)])
        for h in range(NHEAD):
            rb = 64 * (h % 2)
            for half in range(2):
                SY.call("dma_start",
                        out=M["qt_buf"][st % 2][64 * half:64 * half + 64,
                                               h * 256:h * 256 + st_qw(st)],
                        in_=M["qt_s"][h // 2][rb:rb + 64,
                                              st * 256:st * 256 + st_qw(st)],
                        inc=f"sqt{st % 2}", dinc=16)

    def load_kt(ghq):
        if ghq >= NHIDX * 4:
            return
        SY.waitp(rec[("ktfree", ghq - 2)])
        hh = (ghq // 4) % NHEAD
        qq = ghq % 4
        rb = 64 * (hh % 2)
        srcv = M["kt_s"][hh // 2][rb:rb + 64, :].rearrange(
            "r (n two b) -> r n two b", two=2, b=128)
        dst = M["kt_buf"][ghq % 2]
        for par in range(2):
            SY.call("dma_start",
                    out=dst[64 * par:64 * par + 64, :].rearrange(
                        "r (c b) -> r c b", b=128),
                    in_=srcv[:, 8 * qq:8 * qq + 8, par, :],
                    inc=f"skt{ghq % 2}", dinc=16)

    load_qt(0)
    for q0 in range(4):
        load_kt(q0)
    for hidx in range(NHIDX):
        st, h = hidx // NHEAD, hidx % NHEAD
        if h == 4 and st + 1 < NST:
            load_qt(st + 1)
        if hidx >= 1:
            SY.wait("swv", hidx * 2)
        for half2 in range(2):
            SY.call("dma_start", out=M["wvt_buf"][half2][:],
                    in_=M["wvt_s"][h][:, half2 * 4096:(half2 + 1) * 4096],
                    inc=f"swt{half2}", dinc=16)
        for qq in range(4):
            load_kt((hidx + 1) * 4 + qq)
        if h == 5:
            for tile in st_tiles(st):
                if tile >= 2:
                    SY.wait("sxm", 16 * (tile - 1))
                SY.call("dma_start", out=M["xq_buf"][tile % 2][:],
                        in_=M["xq_d"][tile * 128:(tile + 1) * 128, :],
                        inc=f"sxq{tile % 2}", dinc=16)
    for tile in range(NTILE):
        G.waitp(rec[("xmid", tile)])
        G.call("dma_start", out=M["xm_s"][tile], in_=M["xq_buf"][tile % 2][:],
               inc="sxm", dinc=16)
        G.wait("sxm", G.counts["sxm"])

    SY.call("nop", inc="sph")
    SY.wait("sph", 10)
    G.wait("sxm", 16 * NTILE)
    G.call("nop", inc="sph")
    G.wait("sph", 10)

    for i, wsb in enumerate((M["wrT_sb"], M["wkT_sb"], M["wvT_sb"])):
        SY.call("dma_start", out=wsb[:], in_=M["wft_s"][i], inc="lfw", dinc=16)
    for ti in range(NTILE):
        if ti >= 6:
            SY.waitp(rec[("ucopy", ti - 6)])
        SY.call("dma_start", out=M["xmb"][(ti + 4) % 6][:], in_=M["xm_s"][ti],
                inc=f"lxm{(ti + 4) % 6}", dinc=16)
    for g in range(16):
        for ti in range(2):
            tile = 2 * g + 1 + ti
            nld = NTILE + 2 * g + ti
            prev = nld - 6
            if prev < NTILE:
                SY.waitp(rec[("ucopy", prev)])
            else:
                SY.waitp(rec[("yready", prev - NTILE)])
            SY.call("dma_start", out=M["xmb"][(nld + 4) % 6][:],
                    in_=M["xm_s"][tile], inc=f"lxm{(nld + 4) % 6}", dinc=16)
    for g in range(16):
        for ti in range(2):
            G.waitp(rec[("yready", 2 * g + ti)])
            G.call("dma_start",
                   out=M["y_d"][(2 * g + ti) * 128:(2 * g + ti + 1) * 128, :],
                   in_=M["ystage"][ti][:], inc="sst", dinc=16)
            G.wait("sst", G.counts["sst"])
    G.wait("sst", 16 * 32)



def build():
    nc = bass.Bass("TRN2", target_bir_lowering=False)

    M = {}
    M["xb_d"] = nc.dram_tensor("xb", [NK * 128, 1024], F32, kind="ExternalInput").ap()
    M["xq_d"] = nc.dram_tensor("xq", [NTILE * 128, 1024], F32, kind="ExternalInput").ap()
    wq_d = nc.dram_tensor("wq", [512, 1024], F32, kind="ExternalInput").ap()
    wk_d = nc.dram_tensor("wk", [512, 1024], F32, kind="ExternalInput").ap()
    wv_d = nc.dram_tensor("wv", [8192, 1024], F32, kind="ExternalInput").ap()
    wr_d = nc.dram_tensor("wr", [1024, 1024], F32, kind="ExternalInput").ap()
    wkf_d = nc.dram_tensor("wkf", [1024, 1024], F32, kind="ExternalInput").ap()
    wvf_d = nc.dram_tensor("wvf", [1024, 1024], F32, kind="ExternalInput").ap()
    mur_d = nc.dram_tensor("mur", [128, 8], F32, kind="ExternalInput").ap()
    muk_d = nc.dram_tensor("muk", [128, 8], F32, kind="ExternalInput").ap()
    mask_d = nc.dram_tensor("mask", [128, 1], F32, kind="ExternalInput").ap()
    M["y_d"] = nc.dram_tensor("y", [32 * 128, 1024], F32, kind="ExternalOutput").ap()

    M["kt_s"] = nc.dram_tensor("kt_s", [4, 128, 8192], BF16).ap()
    M["qt_s"] = nc.dram_tensor("qt_s", [4, 128, NTILE * 128], BF16).ap()
    M["wvt_s"] = nc.dram_tensor("wvt_s", [8, 128, 8192], BF16).ap()
    M["wft_s"] = nc.dram_tensor("wft_s", [3, 128, 8192], BF16).ap()
    M["xm_s"] = nc.dram_tensor("xm_s", [NTILE, 128, 1024], F32).ap()

    ident_bf_d = nc.inline_tensor(np.eye(128, dtype=BFNP), name="identbf").ap()
    ident_f_d = nc.inline_tensor(np.eye(128, dtype=np.float32), name="identf_d").ap()
    cst = np.zeros((128, 8), np.float32)
    cst[:, 0] = 1e-5
    cst[:, 1] = -0.5
    cst[:, 2] = -1.0
    cst[:, 3] = 0.5
    consts_d = nc.inline_tensor(cst, name="consts_d").ap()

    BASE = 16512

    def at(name, shape, dtype, off):
        return nc.alloc_sbuf_tensor_at(name, shape, dtype, offset=off)

    M["x_ln"] = at("x_ln", [128, NK * 1024], BF16, BASE)
    M["uTall"] = at("uTall", [128, 8 * NTILE * 128], BF16, BASE)
    M["uT"] = [M["uTall"][:, j * NTILE * 128:(j + 1) * NTILE * 128]
               for j in range(8)]
    M["wrT_sb"] = at("wrT", [128, 8192], BF16, BASE + 67584)
    M["wkT_sb"] = at("wkfT", [128, 8192], BF16, BASE + 83968)
    M["wvT_sb"] = at("wvfT", [128, 8192], BF16, BASE + 100352)

    B0 = 147584
    M["consts"] = at("consts", [128, 8], F32, B0)
    M["stats6"] = at("stats6", [128, 48], F32, B0 + 32)
    M["stats2"] = at("stats2", [128, 8], F32, B0 + 224)
    M["nmr"] = at("nmr", [128, 4], F32, B0 + 256)
    M["rstd"] = at("rstd", [128, 4], F32, B0 + 288)
    M["mask_sb"] = at("mask_sb", [128, 1], F32, B0 + 320)
    M["mur_sb"] = at("mur_sb", [128, 8], F32, B0 + 352)
    M["muk_sb"] = at("muk_sb", [128, 8], F32, B0 + 384)
    M["identb"] = at("identb", [128, 128], BF16, B0 + 416)
    M["identf"] = at("identf", [128, 128], F32, B0 + 672)
    BP = B0 + 1184
    TOPB = 229344

    o = BP
    M["wqT_p1"] = at("wqT_p1", [128, 4096], BF16, o); o += 8192
    M["wkT_p1"] = at("wkT_p1", [128, 4096], BF16, o); o += 8192
    OV = o
    M["wldf"] = [at(f"wldf{i}", [128, 1024], F32, OV + i * 4096) for i in range(4)]
    M["wldb"] = [at(f"wldb{i}", [128, 1024], BF16, OV + 16384 + i * 2048)
                 for i in range(4)]
    M["wstage"] = at("wstage", [128, 8192], BF16, OV + 24576)
    M["xf"] = [at(f"xf{i}", [128, 1024], F32, OV + i * 4096) for i in range(4)]
    M["xf"] += [at(f"xf{i}", [128, 1024], F32, BP + 65536 + (i - 4) * 4096)
                for i in range(4, 6)]
    M["xlt"] = [at(f"xlt{i}", [128, 1024], BF16, OV + 16384 + i * 2048)
                for i in range(4)]
    M["ktstage"] = at("ktstage", [128, 4096], BF16, OV + 24576)
    M["qtstage"] = at("qtstage", [128, 1024], BF16, OV + 32768)
    o = OV + 40960
    M["qlnb"] = [at(f"qlnb{i}", [128, 1024], BF16, o + i * 2048) for i in range(4)]
    o += 8192
    assert o <= TOPB, o

    o = BP
    M["kt_buf"] = [at(f"ktb{i}", [128, 1024], BF16, o + i * 2048) for i in range(2)]
    o += 4096
    M["wvt_buf"] = [at(f"wvtb{i}", [128, 4096], BF16, o + i * 8192) for i in range(2)]
    o += 16384
    M["qt_buf"] = [at(f"qtb{i}", [128, 2048], BF16, o + i * 4096) for i in range(2)]
    o += 8192
    M["pt"] = []
    for s in range(NPT):
        M["pt"].append(at(f"pt{s}", [128, 256], BF16, o)); o += 512
    M["xq_buf"] = [at(f"xqb{i}", [128, 1024], F32, o + i * 4096) for i in range(2)]
    o += 8192
    M["yt"] = []
    for c in range(8):
        M["yt"].append(at(f"yt{c}", [128, 256], BF16, o)); o += 512
    M["out_sb"] = [at(f"outsb{i}", [128, 1024], F32, o + i * 4096) for i in range(2)]
    o += 8192
    M["tmpf"] = at("tmpf", [128, 1024], F32, o); o += 4096
    M["dacc"] = at("dacc", [128, 256], F32, o); o += 1024
    M["recip"] = at("recip", [128, 2], F32, o); o += 32
    assert o <= TOPB, o

    o = BP
    M["xmb"] = [at(f"xmb{i}", [128, 1024], F32, o + i * 4096) for i in range(4)]
    M["xmb"] += [at(f"xmb{i}", [128, 1024], F32, BASE + 116736 + (i - 4) * 4096)
                 for i in range(4, 6)]
    o += 16384
    M["ub"] = [at(f"ub{i}", [128, 1024], BF16, o + i * 2048) for i in range(4)]
    o += 8192
    M["xrT"] = [at(f"xrT{i}", [128, 2048], BF16, o + i * 4096) for i in range(2)]
    o += 8192
    M["xkT"] = [at(f"xkT{i}", [128, 2048], BF16, o + i * 4096) for i in range(2)]
    o += 8192
    M["d1"] = at("d1", [128, 256], F32, o); o += 1024
    M["rgate"] = [at(f"rgate{i}", [128, 1024], F32, o + i * 4096) for i in range(2)]
    o += 8192
    M["krelu"] = at("krelu", [128, 2048], BF16, o); o += 4096
    M["kactT"] = at("kactT", [128, 2048], BF16, o); o += 4096
    M["o2"] = [at(f"o2_{i}", [128, 1024], F32, o + i * 4096) for i in range(2)]
    o += 8192
    M["ystage"] = [at(f"ystage{i}", [128, 1024], F32, o + i * 4096) for i in range(2)]
    o += 8192
    assert o <= TOPB, o

    banks = [nc.alloc_psum_tensor(f"pb{i}", [128, 512], F32) for i in range(8)]
    M["banks"] = banks
    M["bank_bf"] = None

    wjobs, wjob_src, wjob_dest = [], [], []
    wjob_prev_write, wmat_writes = [], []
    nw = 0

    def add_matrix(kind, src_d, ntiles, direct_dest=None, scratch_dst=None,
                   blkw=512):
        nonlocal nw
        first = len(wjobs)
        for t in range(ntiles):
            wjobs.append((kind, t))
            wjob_src.append(src_d[t * 128:(t + 1) * 128, :])
            dd = direct_dest if direct_dest is not None else M["wstage"]
            w = blkw if direct_dest is not None else 1024
            v = dd[:].rearrange("p (j2 two q) -> p j2 two q", two=2, q=w)
            wjob_dest.append([v[:, :, par, t * 128:(t + 1) * 128]
                              for par in range(2)])
            wjob_prev_write.append(nw if (t == 0 and scratch_dst is not None
                                          and nw > 0) else None)
        if scratch_dst is not None:
            nw += 1
            wmat_writes.append({"dst": scratch_dst,
                                "last_job": first + ntiles - 1})

    add_matrix("wq", wq_d, 4, direct_dest=M["wqT_p1"], blkw=512)
    add_matrix("wk", wk_d, 4, direct_dest=M["wkT_p1"], blkw=512)
    add_matrix("wr", wr_d, 8, scratch_dst=M["wft_s"][0])
    add_matrix("wkf", wkf_d, 8, scratch_dst=M["wft_s"][1])
    add_matrix("wvf", wvf_d, 8, scratch_dst=M["wft_s"][2])
    for h in range(8):
        add_matrix(f"wv{h}", wv_d[h * 1024:(h + 1) * 1024, :], 8,
                   scratch_dst=M["wvt_s"][h])
    M["wjobs"] = wjobs
    M["wjob_src"] = wjob_src
    M["wjob_destg"] = wjob_dest
    M["wjob_prev_write"] = wjob_prev_write
    M["wmat_writes"] = wmat_writes
    M["NKW0"] = nw
    M["small_loads"] = [
        (M["identb"][:], ident_bf_d),
        (M["identf"][:], ident_f_d),
        (M["consts"][:, 0:8], consts_d),
        (M["mur_sb"][:], mur_d),
        (M["muk_sb"][:], muk_d),
        (M["mask_sb"][:], mask_d),
    ]
    M["NLD0"] = 6 + len(wjobs)
    M["NLD1"] = M["NLD0"] + (NK + NTILE) + 3
    M["NLD2"] = M["NLD1"] + NTILE

    NJOB = len(wjobs)
    rec = Rec()
    for job in range(NJOB):
        for j in range(8):
            rec[("wtr", job, j)] = ("sto", job * 8 + j + 1)
    for j in range(8):
        rec[("p1tr", 0, j)] = ("sto", NJOB * 8 + j + 1)
    for idx in range(1, NK + NTILE):
        for j in range(8):
            rec[("p1tr", idx, j)] = ("sto", NJOB * 8 + 8 + (idx - 1) * 9 + j + 1)
    STO_P3 = NJOB * 8 + (NK + NTILE) * 9
    for ti in range(NTILE):
        for j in range(8):
            rec[("utr", ti, j)] = ("sto", STO_P3 + ti * 8 + j + 1)

    _LN_LAST.clear()
    counts = {n: 0 for n in SEMS}
    emit_tensor(Eng("t", counts, "sto"), M, rec)
    A0 = Eng("a", counts, "safo")
    V0 = Eng("v", counts, "svo")
    emit_scalar_vector(A0, V0, M, rec)
    emit_dma(Eng("s", counts, "ld"), Eng("g", counts, "skw"), M, rec)
    totals = dict(counts)

    _LN_LAST.clear()
    counts2 = {n: 0 for n in SEMS}
    Tr = Eng("t", counts2, "sto", record=True)
    emit_tensor(Tr, M, rec)
    Ar = Eng("a", counts2, "safo", record=True)
    Vr = Eng("v", counts2, "svo", record=True)
    emit_scalar_vector(Ar, Vr, M, rec)
    Sr = Eng("s", counts2, "ld", record=True)
    Gr = Eng("g", counts2, "skw", record=True)
    emit_dma(Sr, Gr, M, rec)
    assert counts2 == totals, (counts2, totals)

    from contextlib import ExitStack
    ctx = ExitStack()
    sems = {n: ctx.enter_context(nc.semaphore(n)) for n in SEMS}

    with nc.Block() as block:
        @block.tensor
        def _(e):
            replay(e, sems, Tr.ops)

        @block.scalar
        def _(e):
            replay(e, sems, Ar.ops)

        @block.vector
        def _(e):
            replay(e, sems, Vr.ops)

        @block.sync
        def _(e):
            replay(e, sems, Sr.ops)

        @block.gpsimd
        def _(e):
            replay(e, sems, Gr.ops)

    return nc



_NC_CACHE = {}


def kernel(L0_attn_x, L0_attn_Wq, L0_attn_Wk, L0_attn_Wv,
           L0_ffn_W_r, L0_ffn_W_k, L0_ffn_W_v, L0_ffn_mu_r, L0_ffn_mu_k):
    x = np.ascontiguousarray(np.asarray(L0_attn_x, dtype=np.float32))
    wq = np.ascontiguousarray(np.asarray(L0_attn_Wq, dtype=np.float32))
    wk = np.ascontiguousarray(np.asarray(L0_attn_Wk, dtype=np.float32))
    wv = np.ascontiguousarray(np.asarray(L0_attn_Wv, dtype=np.float32))
    wr = np.ascontiguousarray(np.asarray(L0_ffn_W_r, dtype=np.float32))
    wkf = np.ascontiguousarray(np.asarray(L0_ffn_W_k, dtype=np.float32))
    wvf = np.ascontiguousarray(np.asarray(L0_ffn_W_v, dtype=np.float32))
    mur = np.asarray(L0_ffn_mu_r, dtype=np.float32).reshape(8, 128).T.copy()
    muk = np.asarray(L0_ffn_mu_k, dtype=np.float32).reshape(8, 128).T.copy()

    if "nc" not in _NC_CACHE:
        _NC_CACHE["nc"] = build()
    nc = _NC_CACHE["nc"]

    in_maps = []
    for core in range(8):
        b, half = core // 2, core % 2
        xb = x[b]
        if half == 0:
            xq = np.concatenate([xb[0:128], xb[0:4096]], axis=0)
            mask = np.zeros((128, 1), np.float32)
        else:
            xq = xb[3968:8192]
            mask = np.ones((128, 1), np.float32)
        in_maps.append(dict(xb=xb, xq=np.ascontiguousarray(xq), wq=wq, wk=wk,
                            wv=wv, wr=wr, wkf=wkf, wvf=wvf, mur=mur, muk=muk,
                            mask=mask))
    res = run_bass_kernel_spmd(nc, in_maps, core_ids=list(range(8)))
    y = np.empty((4, 8192, 1024), np.float32)
    for core in range(8):
        b, half = core // 2, core % 2
        y[b, half * 4096:(half + 1) * 4096, :] = res.results[core]["y"]
    return y


# version 28
# speedup vs baseline: 1.0198x; 1.0198x over previous
import os
import sys
import numpy as np

for _p in ("/opt/trn_rl_repo",):
    if _p not in sys.path and os.path.isdir(_p):
        sys.path.insert(0, _p)

import ml_dtypes
import concourse.bass as bass
import concourse.mybir as mybir
from concourse.bass_utils import run_bass_kernel_spmd

F32 = mybir.dt.float32
BF16 = mybir.dt.bfloat16
AF = mybir.ActivationFunctionType
ALU = mybir.AluOpType
AX = mybir.AxisListType
BFNP = ml_dtypes.bfloat16

NK = 64
NTILE = 33
NST = 17
NHEAD = 8
NPT = 40
NHIDX = NST * NHEAD
SDA_H = NK + 3
SEV_H = 10

SEMS = ["smm", "sex", "sda", "syb", "sye", "sev", "strm", "swv",
        "svo", "safo", "sto", "skw", "sxm", "sst", "sph",
        "lsm", "lw0", "lw1", "lw2", "lw3", "lxf0", "lxf1", "lxf2", "lxf3", "lxf4", "lxf5",
        "lfw", "lxm0", "lxm1", "lxm2", "lxm3", "lxm4", "lxm5",
        "skt0", "skt1", "swt0", "swt1", "sqt0", "sqt1", "sxq0", "sxq1"]

NOINC = "__noinc__"

CORD = [4, 5, 6, 7, 0, 1, 2, 3]
CPOS = {c: i for i, c in enumerate(CORD)}


def st_tiles(st):
    return [2 * st, 2 * st + 1] if st < 16 else [32]


def st_qw(st):
    return 256 if st < 16 else 128


class Rec(dict):
    def __missing__(self, key):
        return None


class Eng:

    def __init__(self, name, counts, fb, record=False):
        self.name = name
        self.counts = counts
        self.fb = fb
        self.waited = {}
        self.prev = None
        self.ops = [] if record else None

    def wait(self, sem, v):
        if v is None or v <= 0:
            return
        if self.waited.get(sem, 0) >= v:
            return
        self.waited[sem] = v
        if self.ops is not None:
            self.ops.append(("w", sem, v))

    def waitp(self, pv):
        if pv is not None:
            self.wait(*pv)

    def call(self, meth, *args, inc=NOINC, dinc=1, chain=False, **kwargs):
        if chain:
            self.waitp(self.prev)
        if inc is NOINC:
            sem = None
        else:
            sem = inc if inc is not None else self.fb
            self.counts[sem] += dinc
            self.prev = (sem, self.counts[sem])
        if self.ops is not None:
            self.ops.append(("c", meth, args, kwargs, sem, dinc))
        return self.prev


def replay(eng, sems, ops):
    for op in ops:
        if op[0] == "w":
            eng.wait_ge(sems[op[1]], op[2])
        else:
            _, meth, args, kwargs, sem, dinc = op
            ins = getattr(eng, meth)(*args, **kwargs)
            if sem is not None:
                ins.then_inc(sems[sem], dinc)



def emit_tensor(T, M, rec):
    banks = M["banks"]
    identb, identf = M["identb"], M["identf"]

    def y_ps(c, qw):
        return banks[2 + (c % 4)][:, (c // 4) * 256:(c // 4) * 256 + qw]

    T.wait("lsm", 16 * 6)
    for job in range(len(M["wjobs"])):
        T.waitp(rec[("wconv", job)])
        for j in range(8):
            if job >= 1:
                T.waitp(rec[("weg", job - 1, j % 2)])
            T.call("transpose",
                   banks[j % 2][:].bitcast(BF16)[:, (j // 2) * 128:(j // 2 + 1) * 128],
                   M["wldb"][job % 4][:, j * 128:(j + 1) * 128], identb[:],
                   inc="sto")
    def p1_tr(idx):
        is_q = idx >= NK
        srct = (M["x_ln"][:, idx * 1024:(idx + 1) * 1024] if not is_q
                else M["qlnb"][idx % 4][:])
        T.waitp(rec[("lncopy", idx)])
        for j in range(8):
            if idx >= 1:
                T.waitp(rec[("p1eg", idx - 1, j % 2)])
            T.call("transpose",
                   banks[2 + (j % 2)][:].bitcast(BF16)[:, (j // 2) * 128:(j // 2 + 1) * 128],
                   srct[:, j * 128:(j + 1) * 128], identb[:], inc="sto")

    def p1_mm(idx):
        is_q = idx >= NK
        T.waitp(rec[("p1eg", idx, 1)])
        if idx >= 4:
            T.waitp(rec[("p1stage", idx - 4)])
        wT = M["wqT_p1"] if is_q else M["wkT_p1"]
        pb = banks[4 + idx % 4]
        for p in range(4):
            for j in range(8):
                last = (p == 3 and j == 7)
                T.call("matmul", pb[:, p * 128:(p + 1) * 128],
                       wT[:, j * 512 + p * 128:j * 512 + (p + 1) * 128],
                       M["xlt"][idx % 4][:, j * 128:(j + 1) * 128],
                       start=(p == 0 and j == 0), stop=last,
                       skip_group_check=True, inc=("sto" if last else NOINC))
        rec[("p1mm", idx)] = T.prev

    p1_tr(0)
    for idx in range(NK + NTILE):
        if idx + 1 < NK + NTILE:
            p1_tr(idx + 1)
        p1_mm(idx)

    T.waitp(T.prev)
    T.call("nop", inc="sph")
    T.wait("sph", 5)

    def spair(hidx, k):
        st, h = hidx // NHEAD, hidx % NHEAD
        qw = st_qw(st)
        ghq = hidx * 4 + k // 16
        T.wait(f"skt{ghq % 2}", 32 * (ghq // 2 + 1))
        T.wait(f"sqt{st % 2}", 256 * (st // 2 + 1))
        T.wait("sex", hidx * NK + k)
        kb = M["kt_buf"][ghq % 2]
        qb = M["qt_buf"][st % 2]
        km = ((k % 16) // 2) * 128
        T.call("matmul", banks[0][:, 0:qw], kb[0:64, km:km + 128],
               qb[0:64, h * 256:h * 256 + qw], start=True, stop=True)
        T.call("matmul", banks[1][:, 0:qw], kb[64:128, km:km + 128],
               qb[64:128, h * 256:h * 256 + qw], start=True, stop=True,
               inc="smm")
        rec[("ktfree", ghq)] = ("smm", T.counts["smm"])
        rec[("qtfree", st)] = ("smm", T.counts["smm"])

    NPAIR = NHIDX * 32

    def spair_g(q):
        if q < NPAIR:
            spair(q // 32, (q % 32) * 2)

    LEAD = 4
    for q in range(8 + LEAD):
        spair_g(q)
    for hidx in range(NHIDX):
        st, h = hidx // NHEAD, hidx % NHEAD
        qw = st_qw(st)
        tiles = st_tiles(st)
        for kg in range(4):
            for ci, c in enumerate(CORD):
                T.wait("sex", hidx * NK + kg * 16 + 16)
                if kg == 3 and ci >= 4:
                    T.wait("sev", hidx * SEV_H + (ci - 4) + 1)
                if kg == 0 and hidx > 0:
                    T.wait("sev", (hidx - 1) * SEV_H + ci + 1
                           if False else (hidx - 1) * SEV_H + CPOS[c] + 1)
                    if ci < 2:
                        T.wait("sda", hidx * SDA_H)
                for kk in range(16):
                    k = kg * 16 + kk
                    gch = hidx * NK + k
                    last = (kk == 15)
                    inc = NOINC
                    if last and kg == 3:
                        inc = "sye"
                    elif last and ci == 7:
                        inc = "syb"
                    T.call("matmul", y_ps(c, qw),
                           M["x_ln"][:, k * 1024 + c * 128:k * 1024 + (c + 1) * 128],
                           M["pt"][gch % NPT][:, 0:qw],
                           start=(kg == 0 and kk == 0 and ci < 4),
                           stop=(kg == 3 and kk == 15),
                           skip_group_check=True, inc=inc)
                bg = (hidx * 4 + kg) * 8 + ci
                spair_g(bg + 8 + LEAD)
        for i in range(2 if qw == 256 else 1):
            T.wait("sev", hidx * SEV_H + 5 + i)
            T.wait("sda", hidx * SDA_H + NK)
            T.call("transpose", banks[2 + i][:, 0:128],
                   M["dacc"][:, i * 128:(i + 1) * 128], identf[:], inc="strm",
                   dinc=(1 if qw == 256 else 2))
        T.wait("swt0", (hidx + 1) * 16)
        T.wait("swt1", (hidx + 1) * 16)
        for ti in range(len(tiles)):
            for c in range(8):
                T.wait("sev", hidx * SEV_H + CPOS[c] + 1)
                if ti == 1 and c == 0:
                    T.wait("sev", hidx * SEV_H + 9)
                half, cc = c // 4, c % 4
                T.call("matmul", banks[6][:, 0:512],
                       M["yt"][c][:, ti * 128:ti * 128 + 128],
                       M["wvt_buf"][half][:, cc * 1024:cc * 1024 + 512],
                       start=(c == 0), stop=(c == 7), skip_group_check=True)
                last = (c == 7)
                T.call("matmul", banks[7][:, 0:512],
                       M["yt"][c][:, ti * 128:ti * 128 + 128],
                       M["wvt_buf"][half][:, cc * 1024 + 512:(cc + 1) * 1024],
                       start=(c == 0), stop=(c == 7), skip_group_check=True,
                       inc=("swv" if last else NOINC),
                       dinc=(2 if (last and len(tiles) == 1) else 1))

    T.waitp(T.prev)
    T.call("nop", inc="sph")
    T.wait("sph", 10)

    for ti in range(NTILE):
        T.waitp(rec[("ucopy", ti)])
        for j in range(8):
            if ti >= 1:
                T.waitp(rec[("ueg", ti - 1, j % 2)])
            T.call("transpose",
                   banks[j % 2][:].bitcast(BF16)[:, (j // 2) * 128:(j // 2 + 1) * 128],
                   M["ub"][ti % 4][:, j * 128:(j + 1) * 128], identb[:], inc="sto")

    T.waitp(rec["p3a_v_done"])
    T.wait("lfw", 16 * 3)
    for g in range(16):
        T.waitp(rec[("mix", g)])
        for ti in range(2):
            T.waitp(rec[("o2", g - 1, ti)])
            for j in range(8):
                T.call("matmul", banks[4 + 2 * ti][:, 0:512],
                       M["xrT"][g % 2][:, j * 256 + ti * 128:j * 256 + ti * 128 + 128],
                       M["wrT_sb"][:, j * 1024:j * 1024 + 512],
                       start=(j == 0), stop=False, skip_group_check=True)
                T.call("matmul", banks[5 + 2 * ti][:, 0:512],
                       M["xrT"][g % 2][:, j * 256 + ti * 128:j * 256 + ti * 128 + 128],
                       M["wrT_sb"][:, j * 1024 + 512:(j + 1) * 1024],
                       start=(j == 0), stop=(j == 7), skip_group_check=True,
                       inc=("strm" if j == 7 else NOINC))
            rec[("m2", g, ti)] = T.prev
        T.waitp(rec[("relu", g - 1)])
        for j in range(8):
            for cj in range(8):
                T.call("matmul",
                       banks[j % 4][:, (j // 4) * 256:(j // 4) * 256 + 256],
                       M["wkT_sb"][:, cj * 1024 + j * 128:cj * 1024 + (j + 1) * 128],
                       M["xkT"][g % 2][:, cj * 256:(cj + 1) * 256],
                       start=(cj == 0 and j < 4), stop=(cj == 7),
                       skip_group_check=True,
                       inc=("strm" if cj == 7 else NOINC))
            rec[("m1", g, j)] = T.prev
        T.waitp(rec[("kact", g)])
        for ti in range(2):
            T.waitp(rec[("sigexp", g, ti)])
            for j in range(8):
                T.call("matmul", banks[4 + 2 * ti][:, 0:512],
                       M["kactT"][:, j * 256 + ti * 128:j * 256 + ti * 128 + 128],
                       M["wvT_sb"][:, j * 1024:j * 1024 + 512],
                       start=(j == 0), stop=False, skip_group_check=True)
                T.call("matmul", banks[5 + 2 * ti][:, 0:512],
                       M["kactT"][:, j * 256 + ti * 128:j * 256 + ti * 128 + 128],
                       M["wvT_sb"][:, j * 1024 + 512:(j + 1) * 1024],
                       start=(j == 0), stop=(j == 7), skip_group_check=True,
                       inc=("strm" if j == 7 else NOINC))
            rec[("m3", g, ti)] = T.prev


_LN_LAST = {}


def emit_ln(V, A, M, src_f32, dst, rec_key=None, rec=None, par=0):
    half = src_f32.shape[-1] // 2
    st6 = M["stats6"][:, par * 12:par * 12 + 12]
    st2 = M["stats2"][:, par * 2:par * 2 + 2]
    nmr = M["nmr"][:, par:par + 1]
    rstd = M["rstd"][:, par:par + 1]
    war = _LN_LAST.get((id(A), par))
    V.call("bn_stats", st6[:, 0:6], src_f32[:, 0:half], chain=True, inc=None)
    V.call("bn_stats", st6[:, 6:12], src_f32[:, half:2 * half],
           chain=True, inc=None)
    if war is not None:
        V.wait("safo", war)
    V.call("bn_aggr", st2, st6, chain=True, inc=None)
    A.waitp(V.prev)
    A.call("activation", rstd, st2[:, 1:2], AF.Ln,
           bias=M["consts"][:, 0:1], chain=True, inc=None)
    A.call("activation", rstd, rstd, AF.Exp,
           scale=M["consts"][:, 1:2], chain=True, inc=None)
    V.waitp(A.prev)
    V.call("tensor_scalar", nmr, st2[:, 0:1], -1.0, rstd,
           ALU.mult, ALU.mult, chain=True, inc=None)
    A.waitp(V.prev)
    A.call("activation", dst, src_f32, AF.Identity, bias=nmr,
           scale=rstd, chain=True, inc=None)
    _LN_LAST[(id(A), par)] = A.counts["safo"]
    if rec_key is not None:
        rec[rec_key] = A.prev


def ln_s1(V, A, M, rec, key, src, par):
    st6 = M["stats6"][:, par * 12:par * 12 + 12]
    war = rec[("lnae", key[0], key[1] - 4)] if key[1] >= 4 else None
    half = src.shape[-1] // 2
    V.call("bn_stats", st6[:, 0:6], src[:, 0:half], chain=True, inc=None)
    V.waitp(war)
    V.call("bn_stats", st6[:, 6:12], src[:, half:2 * half], chain=True, inc=None)
    V.call("bn_aggr", M["stats2"][:, par * 2:par * 2 + 2], st6, chain=True,
           inc=None)
    rec[("lnagg", key[0], key[1])] = V.prev


def ln_s2(V, A, M, rec, key, par):
    A.waitp(rec[("lnagg", key[0], key[1])])
    A.call("activation", M["rstd"][:, par:par + 1],
           M["stats2"][:, par * 2 + 1:par * 2 + 2], AF.Ln,
           bias=M["consts"][:, 0:1], chain=True, inc=None)
    A.call("activation", M["rstd"][:, par:par + 1], M["rstd"][:, par:par + 1],
           AF.Exp, scale=M["consts"][:, 1:2], chain=True, inc=None)
    rec[("lnae", key[0], key[1])] = A.prev


def ln_s3(V, A, M, rec, key, par):
    V.waitp(rec[("lnae", key[0], key[1])])
    if key[1] >= 4:
        V.waitp(rec[("lncp", key[0], key[1] - 4)])
    V.call("tensor_scalar", M["nmr"][:, par:par + 1],
           M["stats2"][:, par * 2:par * 2 + 1], -1.0, M["rstd"][:, par:par + 1],
           ALU.mult, ALU.mult, chain=True, inc=None)
    rec[("lnnm", key[0], key[1])] = V.prev


def ln_s4(V, A, M, rec, key, src, dst, par):
    A.waitp(rec[("lnnm", key[0], key[1])])
    A.call("activation", dst, src, AF.Identity, bias=M["nmr"][:, par:par + 1],
           scale=M["rstd"][:, par:par + 1], chain=True, inc=None)
    rec[("lncp", key[0], key[1])] = A.prev


def emit_scalar_vector(A, V, M, rec):
    banks = M["banks"]

    A.wait("lsm", 16 * 6)
    V.wait("lsm", 16 * 6)

    def p0_conv(job):
        V.wait(f"lw{job % 4}", 16 * (job // 4 + 1))
        if job >= 4:
            V.waitp(rec[("wtr", job - 4, 7)])
        V.call("tensor_copy", M["wldb"][job % 4][:], M["wldf"][job % 4][:],
               chain=True, inc=None)
        rec[("wconv", job)] = V.prev

    for j0 in range(min(3, len(M["wjobs"]))):
        p0_conv(j0)
    for job in range(len(M["wjobs"])):
        if job + 3 < len(M["wjobs"]):
            p0_conv(job + 3)
        if M["wjob_prev_write"][job] is not None:
            V.wait("skw", 16 * M["wjob_prev_write"][job])
        for par in range(2):
            V.waitp(rec[("wtr", job, 6 + par)])
            V.call("tensor_copy", M["wjob_destg"][job][par],
                   M["banks"][par][:].bitcast(BF16)[:, 0:512].rearrange(
                       "p (c b) -> p c b", b=128),
                   chain=True, inc=None)
            rec[("weg", job, par)] = V.prev

    NP1 = NK + NTILE

    def p1_src(i):
        return M["xf"][(i + 4) % 6][:]

    def p1_dst(i):
        return (M["x_ln"][:, i * 1024:(i + 1) * 1024] if i < NK
                else M["qlnb"][i % 4][:])

    def p1_evac(idx):
        for par in range(2):
            V.waitp(rec[("p1tr", idx, 6 + par)])
            V.call("tensor_copy",
                   M["xlt"][idx % 4][:].rearrange("p (j2 two b) -> p j2 two b",
                                                  two=2, b=128)[:, :, par, :],
                   M["banks"][2 + par][:].bitcast(BF16)[:, 0:512].rearrange(
                       "p (c b) -> p c b", b=128),
                   chain=True, inc=None)
            rec[("p1eg", idx, par)] = V.prev

    def p1_stage(idx):
        V.waitp(rec[("p1mm", idx)])
        pbk = banks[4 + idx % 4]
        if idx < NK:
            V.wait("skw", 16 * (M["NKW0"] + 4 * (idx // 8)))
            g = idx % 8
            dst3 = M["ktstage"][:].rearrange("p (a f) -> p a f", a=4)[:, :, g * 128:(g + 1) * 128]
        else:
            qi = idx - NK
            V.wait("skw", 16 * (M["NKW0"] + 32 + 4 * (qi // 2)))
            dst3 = M["qtstage"][:].rearrange("p (a f) -> p a f", a=4)[:, :, (qi % 2) * 128:(qi % 2) * 128 + 128]
        src3 = pbk[:].rearrange("p (a f) -> p a f", a=4)
        V.call("tensor_copy", dst3, src3, chain=True, inc=None)
        rec[("p1stage", idx)] = V.prev

    for i in range(NP1 + 5):
        if i < NP1:
            V.wait(f"lxf{(i + 4) % 6}", 16 * (i // 6 + 1))
            ln_s1(V, A, M, rec, ("p1", i), p1_src(i), i % 4)
        if 0 <= i - 1 < NP1:
            ln_s2(V, A, M, rec, ("p1", i - 1), (i - 1) % 4)
        if 0 <= i - 2 < NP1:
            ln_s3(V, A, M, rec, ("p1", i - 2), (i - 2) % 4)
        if 0 <= i - 3 < NP1:
            j = i - 3
            if j >= NK and j - 4 >= NK:
                A.waitp(rec[("p1tr", j - 4, 7)])
            ln_s4(V, A, M, rec, ("p1", j), p1_src(j), p1_dst(j), j % 4)
            rec[("lncopy", j)] = rec[("lncp", "p1", j)]
        if 0 <= i - 4 < NP1:
            p1_evac(i - 4)
        if 0 <= i - 5 < NP1:
            p1_stage(i - 5)

    A.waitp(A.prev)
    A.call("nop", inc="sph")
    A.wait("sph", 5)
    V.waitp(V.prev)
    V.call("nop", inc="sph")
    V.wait("sph", 5)

    def epilogue(st):
        tiles = st_tiles(st)
        for ii, tile in enumerate(tiles):
            V.wait(f"sxq{tile % 2}", 16 * (tile // 2 + 1))
            if ii == 1:
                A.waitp(rec[("xmid", tiles[0])])
            emit_ln(V, A, M, M["out_sb"][ii][:], M["tmpf"][:],
                    ("epicopy", st, ii), rec, par=ii)
            V.waitp(A.prev)
            V.call("tensor_add", M["xq_buf"][tile % 2][:],
                   M["xq_buf"][tile % 2][:], M["tmpf"][:], chain=True, inc=None)
            rec[("xmid", tile)] = V.prev

    for hidx in range(NHIDX):
        st, h = hidx // NHEAD, hidx % NHEAD
        qw = st_qw(st)
        tiles = st_tiles(st)
        for k in range(NK):
            gch = hidx * NK + k
            A.wait("smm", gch // 2 + 1)
            jj = gch - NPT
            if jj >= 0:
                jh, jk = jj // NK, jj % NK
                kgj = jk // 16
                if kgj == 3:
                    A.wait("sye", jh * 8 + 8)
                else:
                    A.wait("syb", jh * 3 + kgj + 1)
                A.wait("sda", jh * SDA_H + min((kgj + 1) * 16, NK))
            A.call("activation", M["pt"][gch % NPT][:, 0:qw],
                   banks[k % 2][:, 0:qw], AF.Exp, inc="sex")
            if h == 0 and st >= 1 and k == 35:
                pass
        for k in range(NK):
            gch = hidx * NK + k
            V.wait("sex", hidx * NK + k + 1)
            if k == 0:
                V.wait("strm", 2 * hidx)
                V.call("tensor_copy", M["dacc"][:, 0:qw], M["pt"][gch % NPT][:, 0:qw],
                       chain=True, inc="sda")
            else:
                V.call("tensor_add", M["dacc"][:, 0:qw], M["dacc"][:, 0:qw],
                       M["pt"][gch % NPT][:, 0:qw], chain=True, inc="sda")
        if h == 0 and st >= 1:
            epilogue(st - 1)
        for ci, c in enumerate(CORD):
            V.wait("sye", hidx * 8 + ci + 1)
            if hidx > 0 and ci == 0:
                V.wait("swv", hidx * 2)
            V.call("tensor_copy", M["yt"][c][:, 0:qw],
                   banks[2 + (c % 4)][:, (c // 4) * 256:(c // 4) * 256 + qw],
                   chain=True, inc="sev")
        for i in range(2 if qw == 256 else 1):
            V.wait("strm", hidx * 2 + (i + 1 if qw == 256 else 2))
            V.call("reduce_sum", M["recip"][:, i:i + 1],
                   banks[2 + i][:, 0:128], axis=AX.X, chain=True, inc="sda",
                   dinc=(1 if qw == 256 else 2))
        V.call("reciprocal", M["recip"][:, 0:2], M["recip"][:, 0:2],
               chain=True, inc="sda")
        for ti in range(len(tiles)):
            V.wait("swv", hidx * 2 + (ti + 1 if len(tiles) == 2 else 2))
            if h == 0:
                V.waitp(rec[("epicopy", st - 1, ti)])
            ob = M["out_sb"][ti]
            rc = M["recip"][:, ti:ti + 1]
            d2 = 2 if (len(tiles) == 1) else 1
            if h == 0:
                V.call("tensor_scalar_mul", ob[:, 0:512], banks[6][:], rc,
                       chain=True, inc=None)
                V.call("tensor_scalar_mul", ob[:, 512:1024], banks[7][:], rc,
                       chain=True, inc="sev", dinc=d2)
            else:
                V.call("scalar_tensor_tensor", ob[:, 0:512], banks[6][:], rc,
                       ob[:, 0:512], ALU.mult, ALU.add, chain=True, inc=None)
                V.call("scalar_tensor_tensor", ob[:, 512:1024], banks[7][:], rc,
                       ob[:, 512:1024], ALU.mult, ALU.add, chain=True, inc="sev",
                       dinc=d2)

    epilogue(NST - 1)
    A.waitp(A.prev)
    A.call("nop", inc="sph")
    A.wait("sph", 10)
    V.waitp(V.prev)
    V.call("nop", inc="sph")
    V.wait("sph", 10)


    def p3a_evac(ti):
        for par in range(2):
            V.waitp(rec[("utr", ti, 6 + par)])
            V.call("tensor_copy",
                   M["uTall"][:].rearrange("p (j2 two q) -> p j2 two q",
                                           two=2, q=NTILE * 128)[:, :, par,
                                                                 ti * 128:(ti + 1) * 128],
                   M["banks"][par][:].bitcast(BF16)[:, 0:512].rearrange(
                       "p (c b) -> p c b", b=128),
                   chain=True, inc=None)
            rec[("ueg", ti, par)] = V.prev
        if ti == 0:
            for j in range(8):
                V.call("tensor_scalar_mul", M["uT"][j][:, 127:128],
                       M["uT"][j][:, 127:128], M["mask_sb"][:, 0:1],
                       chain=True, inc=None)

    for i in range(NTILE + 6):
        if i < NTILE:
            V.wait(f"lxm{(i + 4) % 6}", 16 * (i // 6 + 1))
            ln_s1(V, A, M, rec, ("ua", i), M["xmb"][(i + 4) % 6][:], i % 4)
        if 0 <= i - 1 < NTILE:
            ln_s2(V, A, M, rec, ("ua", i - 1), (i - 1) % 4)
        if 0 <= i - 2 < NTILE:
            ln_s3(V, A, M, rec, ("ua", i - 2), (i - 2) % 4)
        if 0 <= i - 3 < NTILE:
            j = i - 3
            if j >= 4:
                A.waitp(rec[("utr", j - 4, 7)])
            ln_s4(V, A, M, rec, ("ua", j), M["xmb"][(j + 4) % 6][:],
                  M["ub"][j % 4][:], j % 4)
            rec[("ucopy", j)] = rec[("lncp", "ua", j)]
        if 0 <= i - 6 < NTILE:
            p3a_evac(i - 6)
    rec["p3a_v_done"] = V.prev

    def mixes(g):
        V.waitp(rec[("m2", g - 2, 1)])
        V.waitp(rec[("m1", g - 2, 7)])
        for j in range(8):
            cur = 128 + g * 256
            V.call("tensor_sub", M["d1"][:], M["uT"][j][:, cur:cur + 256],
                   M["uT"][j][:, cur - 1:cur + 255], chain=True, inc=None)
            V.call("scalar_tensor_tensor", M["xrT"][g % 2][:, j * 256:(j + 1) * 256],
                   M["d1"][:], M["mur_sb"][:, j:j + 1],
                   M["uT"][j][:, cur - 1:cur + 255], ALU.mult, ALU.add,
                   chain=True, inc=None)
            V.call("scalar_tensor_tensor", M["xkT"][g % 2][:, j * 256:(j + 1) * 256],
                   M["d1"][:], M["muk_sb"][:, j:j + 1],
                   M["uT"][j][:, cur - 1:cur + 255], ALU.mult, ALU.add,
                   chain=True, inc=None)
        rec[("mix", g)] = V.prev

    mixes(0)
    mixes(1)
    NG = 16
    for g in range(NG + 1):
        ge = g - 1
        if g < NG:
            for ti in range(2):
                A.waitp(rec[("m2", g, ti)])
                A.waitp(rec[("o2", g - 1, ti)])
                A.call("activation", M["rgate"][ti][:, 0:512], banks[4 + 2 * ti][:],
                       AF.Tanh, scale=M["consts"][:, 3:4], chain=True, inc=None)
                A.call("activation", M["rgate"][ti][:, 512:1024], banks[5 + 2 * ti][:],
                       AF.Tanh, scale=M["consts"][:, 3:4], chain=True, inc=None)
                rec[("sigexp", g, ti)] = A.prev
        if ge >= 0:
            for ti in range(2):
                st6 = M["stats6"][:, ti * 12:ti * 12 + 12]
                src = M["o2"][ti][:]
                V.call("bn_stats", st6[:, 0:6], src[:, 0:512], chain=True, inc=None)
                V.call("bn_stats", st6[:, 6:12], src[:, 512:1024],
                       chain=True, inc=None)
                if ge == 0:
                    V.waitp(rec[("lncp", "ua", 32 if ti == 0 else 29)])
                V.waitp(rec[("eplnA", ge - 1, ti)])
                V.call("bn_aggr", M["stats2"][:, ti * 2:ti * 2 + 2], st6,
                       chain=True, inc=None)
                rec[("epagg", ge, ti)] = V.prev
        if g < NG:
            for ti in range(2):
                V.waitp(rec[("sigexp", g, ti)])
                V.call("tensor_scalar_add", M["rgate"][ti][:], M["rgate"][ti][:],
                       1.0, chain=True, inc=None)
                rec[("rgate", g, ti)] = V.prev
            for j in range(8):
                A.waitp(rec[("m1", g, j + 4 if j < 4 else j)])
                if j == 0:
                    A.waitp(rec[("kact", g - 1)])
                A.call("activation", M["krelu"][:, j * 256:(j + 1) * 256],
                       banks[j % 4][:, (j // 4) * 256:(j // 4) * 256 + 256],
                       AF.Relu, chain=True, inc=None)
                rec[("relu1", g, j)] = A.prev
            rec[("relu", g)] = A.prev
            for j in range(8):
                V.waitp(rec[("relu1", g, j)])
                if j == 0:
                    V.waitp(rec[("m3", g - 1, 1)])
                V.call("tensor_mul", M["kactT"][:, j * 256:(j + 1) * 256],
                       M["krelu"][:, j * 256:(j + 1) * 256],
                       M["krelu"][:, j * 256:(j + 1) * 256], chain=True, inc=None)
            rec[("kact", g)] = V.prev
        if ge >= 0:
            for ti in range(2):
                A.waitp(rec[("epagg", ge, ti)])
                if ge == 0:
                    A.waitp(rec[("lnnm", "ua", 32 if ti == 0 else 29)])
                A.waitp(rec[("epnmr", ge - 1, ti)])
                A.call("activation", M["rstd"][:, ti:ti + 1],
                       M["stats2"][:, ti * 2 + 1:ti * 2 + 2], AF.Ln,
                       bias=M["consts"][:, 0:1], chain=True, inc=None)
                A.call("activation", M["rstd"][:, ti:ti + 1],
                       M["rstd"][:, ti:ti + 1], AF.Exp,
                       scale=M["consts"][:, 1:2], chain=True, inc=None)
                rec[("eplnA", ge, ti)] = A.prev
            for ti in range(2):
                V.waitp(rec[("eplnA", ge, ti)])
                V.waitp(rec[("ln3copy", ge - 1, ti)])
                V.call("tensor_scalar", M["nmr"][:, ti:ti + 1],
                       M["stats2"][:, ti * 2:ti * 2 + 1], -1.0,
                       M["rstd"][:, ti:ti + 1], ALU.mult, ALU.mult,
                       chain=True, inc=None)
                rec[("epnmr", ge, ti)] = V.prev
        if g < NG:
            for ti in range(2):
                V.waitp(rec[("m3", g, ti)])
                V.waitp(rec[("rgate", g, ti)])
                V.waitp(rec[("ln3copy", g - 1, ti)])
                V.call("scalar_tensor_tensor", M["o2"][ti][:, 0:512],
                       banks[4 + 2 * ti][:], 0.5, M["rgate"][ti][:, 0:512],
                       ALU.mult, ALU.mult, chain=True, inc=None)
                V.call("scalar_tensor_tensor", M["o2"][ti][:, 512:1024],
                       banks[5 + 2 * ti][:], 0.5, M["rgate"][ti][:, 512:1024],
                       ALU.mult, ALU.mult, chain=True, inc=None)
                rec[("o2", g, ti)] = V.prev
        if ge >= 0:
            for ti in range(2):
                A.waitp(rec[("epnmr", ge, ti)])
                A.wait("sst", 16 * (2 * (ge - 1) + ti + 1))
                A.call("activation", M["ystage"][ti][:], M["o2"][ti][:],
                       AF.Identity, bias=M["nmr"][:, ti:ti + 1],
                       scale=M["rstd"][:, ti:ti + 1], chain=True, inc=None)
                rec[("ln3copy", ge, ti)] = A.prev
                nld = NTILE + 2 * ge + ti
                V.wait(f"lxm{(nld + 4) % 6}", 16 * (nld // 6 + 1))
                V.waitp(rec[("ln3copy", ge, ti)])
                V.call("tensor_add", M["ystage"][ti][:], M["ystage"][ti][:],
                       M["xmb"][(nld + 4) % 6][:], chain=True, inc=None)
                rec[("yready", 2 * ge + ti)] = V.prev
        if g < NG and g + 2 < NG:
            mixes(g + 2)


def emit_dma(SY, G, M, rec):
    def xf_issue(idx):
        src = (M["xb_d"][idx * 128:(idx + 1) * 128, :] if idx < NK
               else M["xq_d"][(idx - NK) * 128:(idx - NK + 1) * 128, :])
        s = (idx + 4) % 6
        SY.call("dma_start", out=M["xf"][s][:], in_=src,
                inc=f"lxf{s}", dinc=16)

    for dst, src in M["small_loads"]:
        SY.call("dma_start", out=dst, in_=src, inc="lsm", dinc=16)
    for idx in range(2):
        xf_issue(idx)
    for job in range(len(M["wjobs"])):
        if job >= 4:
            SY.waitp(rec[("wconv", job - 4)])
        SY.call("dma_start", out=M["wldf"][job % 4][:], in_=M["wjob_src"][job],
                inc=f"lw{job % 4}", dinc=16)
    for idx in range(2, NK + NTILE):
        if idx < 6:
            SY.waitp(rec[("wconv", len(M["wjobs"]) - 1)])
        else:
            SY.waitp(rec[("lncopy", idx - 6)])
        xf_issue(idx)

    for w in M["wmat_writes"]:
        G.waitp(rec[("weg", w["last_job"], 1)])
        G.call("dma_start", out=w["dst"], in_=M["wstage"][:], inc="skw", dinc=16)
        G.wait("skw", G.counts["skw"])
    for gi in range(8):
        G.waitp(rec[("p1stage", gi * 8 + 7)])
        for p in range(4):
            G.call("dma_start",
                   out=M["kt_s"][p][:, gi * 1024:(gi + 1) * 1024],
                   in_=M["ktstage"][:].rearrange("p (a f) -> p a f", a=4)[:, p],
                   inc="skw", dinc=16)
        G.wait("skw", G.counts["skw"])
    for qg in range(17):
        qts = [2 * qg, 2 * qg + 1] if qg < 16 else [32]
        G.waitp(rec[("p1stage", NK + qts[-1])])
        w = len(qts) * 128
        for p in range(4):
            G.call("dma_start",
                   out=M["qt_s"][p][:, qts[0] * 128:qts[0] * 128 + w],
                   in_=M["qtstage"][:].rearrange("p (a f) -> p a f", a=4)[:, p][:, 0:w],
                   inc="skw", dinc=16)
        G.wait("skw", G.counts["skw"])

    SY.call("nop", inc="sph")
    SY.wait("sph", 5)
    G.wait("skw", 16 * (M["NKW0"] + 32 + 68))
    G.call("nop", inc="sph")
    G.wait("sph", 5)

    def load_qt(st):
        SY.waitp(rec[("qtfree", st - 2)])
        for h in range(NHEAD):
            rb = 64 * (h % 2)
            for half in range(2):
                SY.call("dma_start",
                        out=M["qt_buf"][st % 2][64 * half:64 * half + 64,
                                               h * 256:h * 256 + st_qw(st)],
                        in_=M["qt_s"][h // 2][rb:rb + 64,
                                              st * 256:st * 256 + st_qw(st)],
                        inc=f"sqt{st % 2}", dinc=16)

    def load_kt(ghq):
        if ghq >= NHIDX * 4:
            return
        SY.waitp(rec[("ktfree", ghq - 2)])
        hh = (ghq // 4) % NHEAD
        qq = ghq % 4
        rb = 64 * (hh % 2)
        srcv = M["kt_s"][hh // 2][rb:rb + 64, :].rearrange(
            "r (n two b) -> r n two b", two=2, b=128)
        dst = M["kt_buf"][ghq % 2]
        for par in range(2):
            SY.call("dma_start",
                    out=dst[64 * par:64 * par + 64, :].rearrange(
                        "r (c b) -> r c b", b=128),
                    in_=srcv[:, 8 * qq:8 * qq + 8, par, :],
                    inc=f"skt{ghq % 2}", dinc=16)

    load_qt(0)
    for q0 in range(4):
        load_kt(q0)
    for hidx in range(NHIDX):
        st, h = hidx // NHEAD, hidx % NHEAD
        if h == 4 and st + 1 < NST:
            load_qt(st + 1)
        if hidx >= 1:
            SY.wait("swv", hidx * 2)
        for half2 in range(2):
            SY.call("dma_start", out=M["wvt_buf"][half2][:],
                    in_=M["wvt_s"][h][:, half2 * 4096:(half2 + 1) * 4096],
                    inc=f"swt{half2}", dinc=16)
        for qq in range(4):
            load_kt((hidx + 1) * 4 + qq)
        if h == 5:
            for tile in st_tiles(st):
                if tile >= 2:
                    SY.wait("sxm", 16 * (tile - 1))
                SY.call("dma_start", out=M["xq_buf"][tile % 2][:],
                        in_=M["xq_d"][tile * 128:(tile + 1) * 128, :],
                        inc=f"sxq{tile % 2}", dinc=16)
    for tile in range(NTILE):
        G.waitp(rec[("xmid", tile)])
        G.call("dma_start", out=M["xm_s"][tile], in_=M["xq_buf"][tile % 2][:],
               inc="sxm", dinc=16)
        G.wait("sxm", G.counts["sxm"])

    SY.call("nop", inc="sph")
    SY.wait("sph", 10)
    G.wait("sxm", 16 * NTILE)
    G.call("nop", inc="sph")
    G.wait("sph", 10)

    for ti in range(6):
        SY.call("dma_start", out=M["xmb"][(ti + 4) % 6][:], in_=M["xm_s"][ti],
                inc=f"lxm{(ti + 4) % 6}", dinc=16)
    for i, wsb in enumerate((M["wrT_sb"], M["wkT_sb"], M["wvT_sb"])):
        SY.call("dma_start", out=wsb[:], in_=M["wft_s"][i], inc="lfw", dinc=16)
    for ti in range(6, NTILE):
        SY.waitp(rec[("ucopy", ti - 6)])
        SY.call("dma_start", out=M["xmb"][(ti + 4) % 6][:], in_=M["xm_s"][ti],
                inc=f"lxm{(ti + 4) % 6}", dinc=16)
    for g in range(16):
        for ti in range(2):
            tile = 2 * g + 1 + ti
            nld = NTILE + 2 * g + ti
            prev = nld - 6
            if prev < NTILE:
                SY.waitp(rec[("ucopy", prev)])
            else:
                SY.waitp(rec[("yready", prev - NTILE)])
            SY.call("dma_start", out=M["xmb"][(nld + 4) % 6][:],
                    in_=M["xm_s"][tile], inc=f"lxm{(nld + 4) % 6}", dinc=16)
    for g in range(16):
        for ti in range(2):
            G.waitp(rec[("yready", 2 * g + ti)])
            G.call("dma_start",
                   out=M["y_d"][(2 * g + ti) * 128:(2 * g + ti + 1) * 128, :],
                   in_=M["ystage"][ti][:], inc="sst", dinc=16)
            G.wait("sst", G.counts["sst"])
    G.wait("sst", 16 * 32)



def build():
    nc = bass.Bass("TRN2", target_bir_lowering=False)

    M = {}
    M["xb_d"] = nc.dram_tensor("xb", [NK * 128, 1024], F32, kind="ExternalInput").ap()
    M["xq_d"] = nc.dram_tensor("xq", [NTILE * 128, 1024], F32, kind="ExternalInput").ap()
    wq_d = nc.dram_tensor("wq", [512, 1024], F32, kind="ExternalInput").ap()
    wk_d = nc.dram_tensor("wk", [512, 1024], F32, kind="ExternalInput").ap()
    wv_d = nc.dram_tensor("wv", [8192, 1024], F32, kind="ExternalInput").ap()
    wr_d = nc.dram_tensor("wr", [1024, 1024], F32, kind="ExternalInput").ap()
    wkf_d = nc.dram_tensor("wkf", [1024, 1024], F32, kind="ExternalInput").ap()
    wvf_d = nc.dram_tensor("wvf", [1024, 1024], F32, kind="ExternalInput").ap()
    mur_d = nc.dram_tensor("mur", [128, 8], F32, kind="ExternalInput").ap()
    muk_d = nc.dram_tensor("muk", [128, 8], F32, kind="ExternalInput").ap()
    mask_d = nc.dram_tensor("mask", [128, 1], F32, kind="ExternalInput").ap()
    M["y_d"] = nc.dram_tensor("y", [32 * 128, 1024], F32, kind="ExternalOutput").ap()

    M["kt_s"] = nc.dram_tensor("kt_s", [4, 128, 8192], BF16).ap()
    M["qt_s"] = nc.dram_tensor("qt_s", [4, 128, NTILE * 128], BF16).ap()
    M["wvt_s"] = nc.dram_tensor("wvt_s", [8, 128, 8192], BF16).ap()
    M["wft_s"] = nc.dram_tensor("wft_s", [3, 128, 8192], BF16).ap()
    M["xm_s"] = nc.dram_tensor("xm_s", [NTILE, 128, 1024], F32).ap()

    ident_bf_d = nc.inline_tensor(np.eye(128, dtype=BFNP), name="identbf").ap()
    ident_f_d = nc.inline_tensor(np.eye(128, dtype=np.float32), name="identf_d").ap()
    cst = np.zeros((128, 8), np.float32)
    cst[:, 0] = 1e-5
    cst[:, 1] = -0.5
    cst[:, 2] = -1.0
    cst[:, 3] = 0.5
    consts_d = nc.inline_tensor(cst, name="consts_d").ap()

    BASE = 16512

    def at(name, shape, dtype, off):
        return nc.alloc_sbuf_tensor_at(name, shape, dtype, offset=off)

    M["x_ln"] = at("x_ln", [128, NK * 1024], BF16, BASE)
    M["uTall"] = at("uTall", [128, 8 * NTILE * 128], BF16, BASE)
    M["uT"] = [M["uTall"][:, j * NTILE * 128:(j + 1) * NTILE * 128]
               for j in range(8)]
    M["wrT_sb"] = at("wrT", [128, 8192], BF16, BASE + 67584)
    M["wkT_sb"] = at("wkfT", [128, 8192], BF16, BASE + 83968)
    M["wvT_sb"] = at("wvfT", [128, 8192], BF16, BASE + 100352)

    B0 = 147584
    M["consts"] = at("consts", [128, 8], F32, B0)
    M["stats6"] = at("stats6", [128, 48], F32, B0 + 32)
    M["stats2"] = at("stats2", [128, 8], F32, B0 + 224)
    M["nmr"] = at("nmr", [128, 4], F32, B0 + 256)
    M["rstd"] = at("rstd", [128, 4], F32, B0 + 288)
    M["mask_sb"] = at("mask_sb", [128, 1], F32, B0 + 320)
    M["mur_sb"] = at("mur_sb", [128, 8], F32, B0 + 352)
    M["muk_sb"] = at("muk_sb", [128, 8], F32, B0 + 384)
    M["identb"] = at("identb", [128, 128], BF16, B0 + 416)
    M["identf"] = at("identf", [128, 128], F32, B0 + 672)
    BP = B0 + 1184
    TOPB = 229344

    o = BP
    M["wqT_p1"] = at("wqT_p1", [128, 4096], BF16, o); o += 8192
    M["wkT_p1"] = at("wkT_p1", [128, 4096], BF16, o); o += 8192
    OV = o
    M["wldf"] = [at(f"wldf{i}", [128, 1024], F32, OV + i * 4096) for i in range(4)]
    M["wldb"] = [at(f"wldb{i}", [128, 1024], BF16, OV + 16384 + i * 2048)
                 for i in range(4)]
    M["wstage"] = at("wstage", [128, 8192], BF16, OV + 24576)
    M["xf"] = [at(f"xf{i}", [128, 1024], F32, OV + i * 4096) for i in range(4)]
    M["xf"] += [at(f"xf{i}", [128, 1024], F32, BP + 65536 + (i - 4) * 4096)
                for i in range(4, 6)]
    M["xlt"] = [at(f"xlt{i}", [128, 1024], BF16, OV + 16384 + i * 2048)
                for i in range(4)]
    M["ktstage"] = at("ktstage", [128, 4096], BF16, OV + 24576)
    M["qtstage"] = at("qtstage", [128, 1024], BF16, OV + 32768)
    o = OV + 40960
    M["qlnb"] = [at(f"qlnb{i}", [128, 1024], BF16, o + i * 2048) for i in range(4)]
    o += 8192
    assert o <= TOPB, o

    o = BP
    M["kt_buf"] = [at(f"ktb{i}", [128, 1024], BF16, o + i * 2048) for i in range(2)]
    o += 4096
    M["wvt_buf"] = [at(f"wvtb{i}", [128, 4096], BF16, o + i * 8192) for i in range(2)]
    o += 16384
    M["qt_buf"] = [at(f"qtb{i}", [128, 2048], BF16, o + i * 4096) for i in range(2)]
    o += 8192
    M["pt"] = []
    for s in range(NPT):
        M["pt"].append(at(f"pt{s}", [128, 256], BF16, o)); o += 512
    M["xq_buf"] = [at(f"xqb{i}", [128, 1024], F32, o + i * 4096) for i in range(2)]
    o += 8192
    M["yt"] = []
    for c in range(8):
        M["yt"].append(at(f"yt{c}", [128, 256], BF16, o)); o += 512
    M["out_sb"] = [at(f"outsb{i}", [128, 1024], F32, o + i * 4096) for i in range(2)]
    o += 8192
    M["tmpf"] = at("tmpf", [128, 1024], F32, o); o += 4096
    M["dacc"] = at("dacc", [128, 256], F32, o); o += 1024
    M["recip"] = at("recip", [128, 2], F32, o); o += 32
    assert o <= TOPB, o

    o = BP
    M["xmb"] = [at(f"xmb{i}", [128, 1024], F32, o + i * 4096) for i in range(4)]
    M["xmb"] += [at(f"xmb{i}", [128, 1024], F32, BASE + 116736 + (i - 4) * 4096)
                 for i in range(4, 6)]
    o += 16384
    M["ub"] = [at(f"ub{i}", [128, 1024], BF16, o + i * 2048) for i in range(4)]
    o += 8192
    M["xrT"] = [at(f"xrT{i}", [128, 2048], BF16, o + i * 4096) for i in range(2)]
    o += 8192
    M["xkT"] = [at(f"xkT{i}", [128, 2048], BF16, o + i * 4096) for i in range(2)]
    o += 8192
    M["d1"] = at("d1", [128, 256], F32, o); o += 1024
    M["rgate"] = [at(f"rgate{i}", [128, 1024], F32, o + i * 4096) for i in range(2)]
    o += 8192
    M["krelu"] = at("krelu", [128, 2048], BF16, o); o += 4096
    M["kactT"] = at("kactT", [128, 2048], BF16, o); o += 4096
    M["o2"] = [at(f"o2_{i}", [128, 1024], F32, o + i * 4096) for i in range(2)]
    o += 8192
    M["ystage"] = [at(f"ystage{i}", [128, 1024], F32, o + i * 4096) for i in range(2)]
    o += 8192
    assert o <= TOPB, o

    banks = [nc.alloc_psum_tensor(f"pb{i}", [128, 512], F32) for i in range(8)]
    M["banks"] = banks
    M["bank_bf"] = None

    wjobs, wjob_src, wjob_dest = [], [], []
    wjob_prev_write, wmat_writes = [], []
    nw = 0

    def add_matrix(kind, src_d, ntiles, direct_dest=None, scratch_dst=None,
                   blkw=512):
        nonlocal nw
        first = len(wjobs)
        for t in range(ntiles):
            wjobs.append((kind, t))
            wjob_src.append(src_d[t * 128:(t + 1) * 128, :])
            dd = direct_dest if direct_dest is not None else M["wstage"]
            w = blkw if direct_dest is not None else 1024
            v = dd[:].rearrange("p (j2 two q) -> p j2 two q", two=2, q=w)
            wjob_dest.append([v[:, :, par, t * 128:(t + 1) * 128]
                              for par in range(2)])
            wjob_prev_write.append(nw if (t == 0 and scratch_dst is not None
                                          and nw > 0) else None)
        if scratch_dst is not None:
            nw += 1
            wmat_writes.append({"dst": scratch_dst,
                                "last_job": first + ntiles - 1})

    add_matrix("wq", wq_d, 4, direct_dest=M["wqT_p1"], blkw=512)
    add_matrix("wk", wk_d, 4, direct_dest=M["wkT_p1"], blkw=512)
    add_matrix("wr", wr_d, 8, scratch_dst=M["wft_s"][0])
    add_matrix("wkf", wkf_d, 8, scratch_dst=M["wft_s"][1])
    add_matrix("wvf", wvf_d, 8, scratch_dst=M["wft_s"][2])
    for h in range(8):
        add_matrix(f"wv{h}", wv_d[h * 1024:(h + 1) * 1024, :], 8,
                   scratch_dst=M["wvt_s"][h])
    M["wjobs"] = wjobs
    M["wjob_src"] = wjob_src
    M["wjob_destg"] = wjob_dest
    M["wjob_prev_write"] = wjob_prev_write
    M["wmat_writes"] = wmat_writes
    M["NKW0"] = nw
    M["small_loads"] = [
        (M["identb"][:], ident_bf_d),
        (M["identf"][:], ident_f_d),
        (M["consts"][:, 0:8], consts_d),
        (M["mur_sb"][:], mur_d),
        (M["muk_sb"][:], muk_d),
        (M["mask_sb"][:], mask_d),
    ]
    M["NLD0"] = 6 + len(wjobs)
    M["NLD1"] = M["NLD0"] + (NK + NTILE) + 3
    M["NLD2"] = M["NLD1"] + NTILE

    NJOB = len(wjobs)
    rec = Rec()
    for job in range(NJOB):
        for j in range(8):
            rec[("wtr", job, j)] = ("sto", job * 8 + j + 1)
    for j in range(8):
        rec[("p1tr", 0, j)] = ("sto", NJOB * 8 + j + 1)
    for idx in range(1, NK + NTILE):
        for j in range(8):
            rec[("p1tr", idx, j)] = ("sto", NJOB * 8 + 8 + (idx - 1) * 9 + j + 1)
    STO_P3 = NJOB * 8 + (NK + NTILE) * 9
    for ti in range(NTILE):
        for j in range(8):
            rec[("utr", ti, j)] = ("sto", STO_P3 + ti * 8 + j + 1)

    _LN_LAST.clear()
    counts = {n: 0 for n in SEMS}
    emit_tensor(Eng("t", counts, "sto"), M, rec)
    A0 = Eng("a", counts, "safo")
    V0 = Eng("v", counts, "svo")
    emit_scalar_vector(A0, V0, M, rec)
    emit_dma(Eng("s", counts, "ld"), Eng("g", counts, "skw"), M, rec)
    totals = dict(counts)

    _LN_LAST.clear()
    counts2 = {n: 0 for n in SEMS}
    Tr = Eng("t", counts2, "sto", record=True)
    emit_tensor(Tr, M, rec)
    Ar = Eng("a", counts2, "safo", record=True)
    Vr = Eng("v", counts2, "svo", record=True)
    emit_scalar_vector(Ar, Vr, M, rec)
    Sr = Eng("s", counts2, "ld", record=True)
    Gr = Eng("g", counts2, "skw", record=True)
    emit_dma(Sr, Gr, M, rec)
    assert counts2 == totals, (counts2, totals)

    from contextlib import ExitStack
    ctx = ExitStack()
    sems = {n: ctx.enter_context(nc.semaphore(n)) for n in SEMS}

    with nc.Block() as block:
        @block.tensor
        def _(e):
            replay(e, sems, Tr.ops)

        @block.scalar
        def _(e):
            replay(e, sems, Ar.ops)

        @block.vector
        def _(e):
            replay(e, sems, Vr.ops)

        @block.sync
        def _(e):
            replay(e, sems, Sr.ops)

        @block.gpsimd
        def _(e):
            replay(e, sems, Gr.ops)

    return nc



_NC_CACHE = {}


def kernel(L0_attn_x, L0_attn_Wq, L0_attn_Wk, L0_attn_Wv,
           L0_ffn_W_r, L0_ffn_W_k, L0_ffn_W_v, L0_ffn_mu_r, L0_ffn_mu_k):
    x = np.ascontiguousarray(np.asarray(L0_attn_x, dtype=np.float32))
    wq = np.ascontiguousarray(np.asarray(L0_attn_Wq, dtype=np.float32))
    wk = np.ascontiguousarray(np.asarray(L0_attn_Wk, dtype=np.float32))
    wv = np.ascontiguousarray(np.asarray(L0_attn_Wv, dtype=np.float32))
    wr = np.ascontiguousarray(np.asarray(L0_ffn_W_r, dtype=np.float32))
    wkf = np.ascontiguousarray(np.asarray(L0_ffn_W_k, dtype=np.float32))
    wvf = np.ascontiguousarray(np.asarray(L0_ffn_W_v, dtype=np.float32))
    mur = np.asarray(L0_ffn_mu_r, dtype=np.float32).reshape(8, 128).T.copy()
    muk = np.asarray(L0_ffn_mu_k, dtype=np.float32).reshape(8, 128).T.copy()

    if "nc" not in _NC_CACHE:
        _NC_CACHE["nc"] = build()
    nc = _NC_CACHE["nc"]

    in_maps = []
    for core in range(8):
        b, half = core // 2, core % 2
        xb = x[b]
        if half == 0:
            xq = np.concatenate([xb[0:128], xb[0:4096]], axis=0)
            mask = np.zeros((128, 1), np.float32)
        else:
            xq = xb[3968:8192]
            mask = np.ones((128, 1), np.float32)
        in_maps.append(dict(xb=xb, xq=np.ascontiguousarray(xq), wq=wq, wk=wk,
                            wv=wv, wr=wr, wkf=wkf, wvf=wvf, mur=mur, muk=muk,
                            mask=mask))
    res = run_bass_kernel_spmd(nc, in_maps, core_ids=list(range(8)))
    y = np.empty((4, 8192, 1024), np.float32)
    for core in range(8):
        b, half = core // 2, core % 2
        y[b, half * 4096:(half + 1) * 4096, :] = res.results[core]["y"]
    return y


# version 29 (final)
# speedup vs baseline: 1.0646x; 1.0439x over previous
import os
import sys
import numpy as np

for _p in ("/opt/trn_rl_repo",):
    if _p not in sys.path and os.path.isdir(_p):
        sys.path.insert(0, _p)

import ml_dtypes
import concourse.bass as bass
import concourse.mybir as mybir
from concourse.bass_utils import run_bass_kernel_spmd

F32 = mybir.dt.float32
BF16 = mybir.dt.bfloat16
AF = mybir.ActivationFunctionType
ALU = mybir.AluOpType
AX = mybir.AxisListType
BFNP = ml_dtypes.bfloat16

NK = 64
NTILE = 33
NST = 17
NHEAD = 8
NPT = 40
NHIDX = NST * NHEAD
SDA_H = NK + 3
SEV_H = 10

SEMS = ["smm", "sex", "sda", "syb", "sye", "sev", "strm", "swv",
        "svo", "safo", "sto", "skw", "sxm", "sst", "sph",
        "lsm", "lw0", "lw1", "lw2", "lw3", "lxf0", "lxf1", "lxf2", "lxf3", "lxf4", "lxf5",
        "lfw", "lxm0", "lxm1", "lxm2", "lxm3", "lxm4", "lxm5",
        "skt0", "skt1", "swt0", "swt1", "sqt0", "sqt1", "sxq0", "sxq1"]

NOINC = "__noinc__"

CORD = [4, 5, 6, 7, 0, 1, 2, 3]
CPOS = {c: i for i, c in enumerate(CORD)}


def st_tiles(st):
    return [2 * st, 2 * st + 1] if st < 16 else [32]


def st_qw(st):
    return 256 if st < 16 else 128


class Rec(dict):
    def __missing__(self, key):
        return None


class Eng:

    def __init__(self, name, counts, fb, record=False):
        self.name = name
        self.counts = counts
        self.fb = fb
        self.waited = {}
        self.prev = None
        self.ops = [] if record else None

    def wait(self, sem, v):
        if v is None or v <= 0:
            return
        if self.waited.get(sem, 0) >= v:
            return
        self.waited[sem] = v
        if self.ops is not None:
            self.ops.append(("w", sem, v))

    def waitp(self, pv):
        if pv is not None:
            self.wait(*pv)

    def call(self, meth, *args, inc=NOINC, dinc=1, chain=False, **kwargs):
        if chain:
            self.waitp(self.prev)
        if inc is NOINC:
            sem = None
        else:
            sem = inc if inc is not None else self.fb
            self.counts[sem] += dinc
            self.prev = (sem, self.counts[sem])
        if self.ops is not None:
            self.ops.append(("c", meth, args, kwargs, sem, dinc))
        return self.prev


def replay(eng, sems, ops):
    for op in ops:
        if op[0] == "w":
            eng.wait_ge(sems[op[1]], op[2])
        else:
            _, meth, args, kwargs, sem, dinc = op
            ins = getattr(eng, meth)(*args, **kwargs)
            if sem is not None:
                ins.then_inc(sems[sem], dinc)



def emit_tensor(T, M, rec):
    banks = M["banks"]
    identb, identf = M["identb"], M["identf"]

    def y_ps(c, qw):
        return banks[2 + (c % 4)][:, (c // 4) * 256:(c // 4) * 256 + qw]

    T.wait("lsm", 16 * 6)
    for job in range(len(M["wjobs"])):
        T.waitp(rec[("wconv", job)])
        for j in range(8):
            if job >= 1:
                T.waitp(rec[("weg", job - 1, j % 2)])
            T.call("transpose",
                   banks[j % 2][:].bitcast(BF16)[:, (j // 2) * 128:(j // 2 + 1) * 128],
                   M["wldb"][job % 4][:, j * 128:(j + 1) * 128], identb[:],
                   inc="sto")
    def p1_tr(idx):
        is_q = idx >= NK
        srct = (M["x_ln"][:, idx * 1024:(idx + 1) * 1024] if not is_q
                else M["qlnb"][idx % 4][:])
        T.waitp(rec[("lncopy", idx)])
        for j in range(8):
            if idx >= 1:
                T.waitp(rec[("p1eg", idx - 1, j % 2)])
            T.call("transpose",
                   banks[2 + (j % 2)][:].bitcast(BF16)[:, (j // 2) * 128:(j // 2 + 1) * 128],
                   srct[:, j * 128:(j + 1) * 128], identb[:], inc="sto")

    def p1_mm(idx):
        is_q = idx >= NK
        T.waitp(rec[("p1eg", idx, 1)])
        if idx >= 4:
            T.waitp(rec[("p1stage", idx - 4)])
        wT = M["wqT_p1"] if is_q else M["wkT_p1"]
        pb = banks[4 + idx % 4]
        for p in range(4):
            for j in range(8):
                last = (p == 3 and j == 7)
                T.call("matmul", pb[:, p * 128:(p + 1) * 128],
                       wT[:, j * 512 + p * 128:j * 512 + (p + 1) * 128],
                       M["xlt"][idx % 4][:, j * 128:(j + 1) * 128],
                       start=(p == 0 and j == 0), stop=last,
                       skip_group_check=True, inc=("sto" if last else NOINC))
        rec[("p1mm", idx)] = T.prev

    p1_tr(0)
    for idx in range(NK + NTILE):
        if idx + 1 < NK + NTILE:
            p1_tr(idx + 1)
        p1_mm(idx)

    T.waitp(T.prev)
    T.call("nop", inc="sph")
    T.wait("sph", 5)

    def spair(hidx, k):
        st, h = hidx // NHEAD, hidx % NHEAD
        qw = st_qw(st)
        ghq = hidx * 4 + k // 16
        T.wait(f"skt{ghq % 2}", 32 * (ghq // 2 + 1))
        T.wait(f"sqt{st % 2}", 256 * (st // 2 + 1))
        T.wait("sex", hidx * NK + k)
        kb = M["kt_buf"][ghq % 2]
        qb = M["qt_buf"][st % 2]
        km = ((k % 16) // 2) * 128
        T.call("matmul", banks[0][:, 0:qw], kb[0:64, km:km + 128],
               qb[0:64, h * 256:h * 256 + qw], start=True, stop=True)
        T.call("matmul", banks[1][:, 0:qw], kb[64:128, km:km + 128],
               qb[64:128, h * 256:h * 256 + qw], start=True, stop=True,
               inc="smm")
        rec[("ktfree", ghq)] = ("smm", T.counts["smm"])
        rec[("qtfree", st)] = ("smm", T.counts["smm"])

    NPAIR = NHIDX * 32

    def spair_g(q):
        if q < NPAIR:
            spair(q // 32, (q % 32) * 2)

    LEAD = 4
    for q in range(8 + LEAD):
        spair_g(q)
    for hidx in range(NHIDX):
        st, h = hidx // NHEAD, hidx % NHEAD
        qw = st_qw(st)
        tiles = st_tiles(st)
        for kg in range(4):
            for ci, c in enumerate(CORD):
                T.wait("sex", hidx * NK + kg * 16 + 16)
                if kg == 3 and ci >= 4:
                    T.wait("sev", hidx * SEV_H + (ci - 4) + 1)
                if kg == 0 and hidx > 0:
                    T.wait("sev", (hidx - 1) * SEV_H + ci + 1
                           if False else (hidx - 1) * SEV_H + CPOS[c] + 1)
                    if ci < 2:
                        T.wait("sda", hidx * SDA_H)
                for kk in range(16):
                    k = kg * 16 + kk
                    gch = hidx * NK + k
                    last = (kk == 15)
                    inc = NOINC
                    if last and kg == 3:
                        inc = "sye"
                    elif last and ci == 7:
                        inc = "syb"
                    T.call("matmul", y_ps(c, qw),
                           M["x_ln"][:, k * 1024 + c * 128:k * 1024 + (c + 1) * 128],
                           M["pt"][gch % NPT][:, 0:qw],
                           start=(kg == 0 and kk == 0 and ci < 4),
                           stop=(kg == 3 and kk == 15),
                           skip_group_check=True, inc=inc)
                bg = (hidx * 4 + kg) * 8 + ci
                spair_g(bg + 8 + LEAD)
        for i in range(2 if qw == 256 else 1):
            T.wait("sev", hidx * SEV_H + 5 + i)
            T.wait("sda", hidx * SDA_H + NK)
            T.call("transpose", banks[2 + i][:, 0:128],
                   M["dacc"][:, i * 128:(i + 1) * 128], identf[:], inc="strm",
                   dinc=(1 if qw == 256 else 2))
        T.wait("swt0", (hidx + 1) * 16)
        T.wait("swt1", (hidx + 1) * 16)
        for ti in range(len(tiles)):
            for c in range(8):
                T.wait("sev", hidx * SEV_H + CPOS[c] + 1)
                if ti == 1 and c == 0:
                    T.wait("sev", hidx * SEV_H + 9)
                half, cc = c // 4, c % 4
                T.call("matmul", banks[6][:, 0:512],
                       M["yt"][c][:, ti * 128:ti * 128 + 128],
                       M["wvt_buf"][half][:, cc * 1024:cc * 1024 + 512],
                       start=(c == 0), stop=(c == 7), skip_group_check=True)
                last = (c == 7)
                T.call("matmul", banks[7][:, 0:512],
                       M["yt"][c][:, ti * 128:ti * 128 + 128],
                       M["wvt_buf"][half][:, cc * 1024 + 512:(cc + 1) * 1024],
                       start=(c == 0), stop=(c == 7), skip_group_check=True,
                       inc=("swv" if last else NOINC),
                       dinc=(2 if (last and len(tiles) == 1) else 1))

    T.waitp(T.prev)
    T.call("nop", inc="sph")
    T.wait("sph", 10)

    for ti in range(NTILE):
        T.waitp(rec[("ucopy", ti)])
        for j in range(8):
            if ti >= 1:
                T.waitp(rec[("ueg", ti - 1, j % 2)])
            T.call("transpose",
                   banks[j % 2][:].bitcast(BF16)[:, (j // 2) * 128:(j // 2 + 1) * 128],
                   M["ub"][ti % 4][:, j * 128:(j + 1) * 128], identb[:], inc="sto")

    T.waitp(rec["p3a_v_done"])
    T.wait("lfw", 16 * 3)
    for g in range(16):
        T.waitp(rec[("mix", g)])
        for ti in range(2):
            T.waitp(rec[("o2", g - 1, ti)])
            for j in range(8):
                T.call("matmul", banks[4 + 2 * ti][:, 0:512],
                       M["xrT"][g % 2][:, j * 256 + ti * 128:j * 256 + ti * 128 + 128],
                       M["wrT_sb"][:, j * 1024:j * 1024 + 512],
                       start=(j == 0), stop=False, skip_group_check=True)
                T.call("matmul", banks[5 + 2 * ti][:, 0:512],
                       M["xrT"][g % 2][:, j * 256 + ti * 128:j * 256 + ti * 128 + 128],
                       M["wrT_sb"][:, j * 1024 + 512:(j + 1) * 1024],
                       start=(j == 0), stop=(j == 7), skip_group_check=True,
                       inc=("strm" if j == 7 else NOINC))
            rec[("m2", g, ti)] = T.prev
        T.waitp(rec[("relu", g - 1)])
        for j in range(8):
            for cj in range(8):
                T.call("matmul",
                       banks[j % 4][:, (j // 4) * 256:(j // 4) * 256 + 256],
                       M["wkT_sb"][:, cj * 1024 + j * 128:cj * 1024 + (j + 1) * 128],
                       M["xkT"][g % 2][:, cj * 256:(cj + 1) * 256],
                       start=(cj == 0 and j < 4), stop=(cj == 7),
                       skip_group_check=True,
                       inc=("strm" if cj == 7 else NOINC))
            rec[("m1", g, j)] = T.prev
        T.waitp(rec[("kact", g)])
        for ti in range(2):
            T.waitp(rec[("sigexp", g, ti)])
            for j in range(8):
                T.call("matmul", banks[4 + 2 * ti][:, 0:512],
                       M["kactT"][:, j * 256 + ti * 128:j * 256 + ti * 128 + 128],
                       M["wvT_sb"][:, j * 1024:j * 1024 + 512],
                       start=(j == 0), stop=False, skip_group_check=True)
                T.call("matmul", banks[5 + 2 * ti][:, 0:512],
                       M["kactT"][:, j * 256 + ti * 128:j * 256 + ti * 128 + 128],
                       M["wvT_sb"][:, j * 1024 + 512:(j + 1) * 1024],
                       start=(j == 0), stop=(j == 7), skip_group_check=True,
                       inc=("strm" if j == 7 else NOINC))
            rec[("m3", g, ti)] = T.prev


_LN_LAST = {}


def emit_ln(V, A, M, src_f32, dst, rec_key=None, rec=None, par=0):
    half = src_f32.shape[-1] // 2
    st6 = M["stats6"][:, par * 12:par * 12 + 12]
    st2 = M["stats2"][:, par * 2:par * 2 + 2]
    nmr = M["nmr"][:, par:par + 1]
    rstd = M["rstd"][:, par:par + 1]
    war = _LN_LAST.get((id(A), par))
    V.call("bn_stats", st6[:, 0:6], src_f32[:, 0:half], chain=True, inc=None)
    V.call("bn_stats", st6[:, 6:12], src_f32[:, half:2 * half],
           chain=True, inc=None)
    if war is not None:
        V.wait("safo", war)
    V.call("bn_aggr", st2, st6, chain=True, inc=None)
    A.waitp(V.prev)
    A.call("activation", rstd, st2[:, 1:2], AF.Ln,
           bias=M["consts"][:, 0:1], chain=True, inc=None)
    A.call("activation", rstd, rstd, AF.Exp,
           scale=M["consts"][:, 1:2], chain=True, inc=None)
    V.waitp(A.prev)
    V.call("tensor_scalar", nmr, st2[:, 0:1], -1.0, rstd,
           ALU.mult, ALU.mult, chain=True, inc=None)
    A.waitp(V.prev)
    A.call("activation", dst, src_f32, AF.Identity, bias=nmr,
           scale=rstd, chain=True, inc=None)
    _LN_LAST[(id(A), par)] = A.counts["safo"]
    if rec_key is not None:
        rec[rec_key] = A.prev


def ln_s1(V, A, M, rec, key, src, par):
    st6 = M["stats6"][:, par * 12:par * 12 + 12]
    war = rec[("lnae", key[0], key[1] - 4)] if key[1] >= 4 else None
    half = src.shape[-1] // 2
    V.call("bn_stats", st6[:, 0:6], src[:, 0:half], chain=True, inc=None)
    V.waitp(war)
    V.call("bn_stats", st6[:, 6:12], src[:, half:2 * half], chain=True, inc=None)
    V.call("bn_aggr", M["stats2"][:, par * 2:par * 2 + 2], st6, chain=True,
           inc=None)
    rec[("lnagg", key[0], key[1])] = V.prev


def ln_s2(V, A, M, rec, key, par):
    A.waitp(rec[("lnagg", key[0], key[1])])
    A.call("activation", M["rstd"][:, par:par + 1],
           M["stats2"][:, par * 2 + 1:par * 2 + 2], AF.Ln,
           bias=M["consts"][:, 0:1], chain=True, inc=None)
    A.call("activation", M["rstd"][:, par:par + 1], M["rstd"][:, par:par + 1],
           AF.Exp, scale=M["consts"][:, 1:2], chain=True, inc=None)
    rec[("lnae", key[0], key[1])] = A.prev


def ln_s3(V, A, M, rec, key, par):
    V.waitp(rec[("lnae", key[0], key[1])])
    if key[1] >= 4:
        V.waitp(rec[("lncp", key[0], key[1] - 4)])
    V.call("tensor_scalar", M["nmr"][:, par:par + 1],
           M["stats2"][:, par * 2:par * 2 + 1], -1.0, M["rstd"][:, par:par + 1],
           ALU.mult, ALU.mult, chain=True, inc=None)
    rec[("lnnm", key[0], key[1])] = V.prev


def ln_s4(V, A, M, rec, key, src, dst, par):
    A.waitp(rec[("lnnm", key[0], key[1])])
    A.call("activation", dst, src, AF.Identity, bias=M["nmr"][:, par:par + 1],
           scale=M["rstd"][:, par:par + 1], chain=True, inc=None)
    rec[("lncp", key[0], key[1])] = A.prev


def emit_scalar_vector(A, V, M, rec):
    banks = M["banks"]

    A.wait("lsm", 16 * 6)
    V.wait("lsm", 16 * 6)

    def p0_conv(job):
        V.wait(f"lw{job % 4}", 16 * (job // 4 + 1))
        if job >= 4:
            V.waitp(rec[("wtr", job - 4, 7)])
        V.call("tensor_copy", M["wldb"][job % 4][:], M["wldf"][job % 4][:],
               chain=True, inc=None)
        rec[("wconv", job)] = V.prev

    for j0 in range(min(3, len(M["wjobs"]))):
        p0_conv(j0)
    for job in range(len(M["wjobs"])):
        if job + 3 < len(M["wjobs"]):
            p0_conv(job + 3)
        if M["wjob_prev_write"][job] is not None:
            V.wait("skw", 16 * M["wjob_prev_write"][job])
        for par in range(2):
            V.waitp(rec[("wtr", job, 6 + par)])
            V.call("tensor_copy", M["wjob_destg"][job][par],
                   M["banks"][par][:].bitcast(BF16)[:, 0:512].rearrange(
                       "p (c b) -> p c b", b=128),
                   chain=True, inc=None)
            rec[("weg", job, par)] = V.prev

    NP1 = NK + NTILE

    def p1_src(i):
        return M["xf"][(i + 4) % 6][:]

    def p1_dst(i):
        return (M["x_ln"][:, i * 1024:(i + 1) * 1024] if i < NK
                else M["qlnb"][i % 4][:])

    def p1_evac(idx):
        for par in range(2):
            V.waitp(rec[("p1tr", idx, 6 + par)])
            V.call("tensor_copy",
                   M["xlt"][idx % 4][:].rearrange("p (j2 two b) -> p j2 two b",
                                                  two=2, b=128)[:, :, par, :],
                   M["banks"][2 + par][:].bitcast(BF16)[:, 0:512].rearrange(
                       "p (c b) -> p c b", b=128),
                   chain=True, inc=None)
            rec[("p1eg", idx, par)] = V.prev

    def p1_stage(idx):
        V.waitp(rec[("p1mm", idx)])
        pbk = banks[4 + idx % 4]
        if idx < NK:
            V.wait("skw", 16 * (M["NKW0"] + 4 * (idx // 8)))
            g = idx % 8
            dst3 = M["ktstage"][:].rearrange("p (a f) -> p a f", a=4)[:, :, g * 128:(g + 1) * 128]
        else:
            qi = idx - NK
            V.wait("skw", 16 * (M["NKW0"] + 32 + 4 * (qi // 2)))
            dst3 = M["qtstage"][:].rearrange("p (a f) -> p a f", a=4)[:, :, (qi % 2) * 128:(qi % 2) * 128 + 128]
        src3 = pbk[:].rearrange("p (a f) -> p a f", a=4)
        V.call("tensor_copy", dst3, src3, chain=True, inc=None)
        rec[("p1stage", idx)] = V.prev

    for i in range(NP1 + 5):
        if i < NP1:
            V.wait(f"lxf{(i + 4) % 6}", 16 * (i // 6 + 1))
            ln_s1(V, A, M, rec, ("p1", i), p1_src(i), i % 4)
        if 0 <= i - 1 < NP1:
            ln_s2(V, A, M, rec, ("p1", i - 1), (i - 1) % 4)
        if 0 <= i - 2 < NP1:
            ln_s3(V, A, M, rec, ("p1", i - 2), (i - 2) % 4)
        if 0 <= i - 3 < NP1:
            j = i - 3
            if j >= NK and j - 4 >= NK:
                A.waitp(rec[("p1tr", j - 4, 7)])
            ln_s4(V, A, M, rec, ("p1", j), p1_src(j), p1_dst(j), j % 4)
            rec[("lncopy", j)] = rec[("lncp", "p1", j)]
        if 0 <= i - 4 < NP1:
            p1_evac(i - 4)
        if 0 <= i - 5 < NP1:
            p1_stage(i - 5)

    A.waitp(A.prev)
    A.call("nop", inc="sph")
    A.wait("sph", 5)
    V.waitp(V.prev)
    V.call("nop", inc="sph")
    V.wait("sph", 5)

    def epilogue(st):
        tiles = st_tiles(st)
        for ii, tile in enumerate(tiles):
            V.wait(f"sxq{tile % 2}", 16 * (tile // 2 + 1))
            if ii == 1:
                A.waitp(rec[("xmid", tiles[0])])
            emit_ln(V, A, M, M["out_sb"][ii][:], M["tmpf"][:],
                    ("epicopy", st, ii), rec, par=ii)
            V.waitp(A.prev)
            V.call("tensor_add", M["xq_buf"][tile % 2][:],
                   M["xq_buf"][tile % 2][:], M["tmpf"][:], chain=True, inc=None)
            rec[("xmid", tile)] = V.prev

    for hidx in range(NHIDX):
        st, h = hidx // NHEAD, hidx % NHEAD
        qw = st_qw(st)
        tiles = st_tiles(st)
        for k in range(NK):
            gch = hidx * NK + k
            A.wait("smm", gch // 2 + 1)
            jj = gch - NPT
            if jj >= 0:
                jh, jk = jj // NK, jj % NK
                kgj = jk // 16
                if kgj == 3:
                    A.wait("sye", jh * 8 + 8)
                else:
                    A.wait("syb", jh * 3 + kgj + 1)
                A.wait("sda", jh * SDA_H + min((kgj + 1) * 16, NK))
            A.call("activation", M["pt"][gch % NPT][:, 0:qw],
                   banks[k % 2][:, 0:qw], AF.Exp, inc="sex")
            if h == 0 and st >= 1 and k == 35:
                pass
        for k in range(NK):
            gch = hidx * NK + k
            V.wait("sex", hidx * NK + k + 1)
            if k == 0:
                V.wait("strm", 2 * hidx)
                V.call("tensor_copy", M["dacc"][:, 0:qw], M["pt"][gch % NPT][:, 0:qw],
                       chain=True, inc="sda")
            else:
                V.call("tensor_add", M["dacc"][:, 0:qw], M["dacc"][:, 0:qw],
                       M["pt"][gch % NPT][:, 0:qw], chain=True, inc="sda")
        if h == 0 and st >= 1:
            epilogue(st - 1)
        for ci, c in enumerate(CORD):
            V.wait("sye", hidx * 8 + ci + 1)
            if hidx > 0 and ci == 0:
                V.wait("swv", hidx * 2)
            V.call("tensor_copy", M["yt"][c][:, 0:qw],
                   banks[2 + (c % 4)][:, (c // 4) * 256:(c // 4) * 256 + qw],
                   chain=True, inc="sev")
        for i in range(2 if qw == 256 else 1):
            V.wait("strm", hidx * 2 + (i + 1 if qw == 256 else 2))
            V.call("reduce_sum", M["recip"][:, i:i + 1],
                   banks[2 + i][:, 0:128], axis=AX.X, chain=True, inc="sda",
                   dinc=(1 if qw == 256 else 2))
        V.call("reciprocal", M["recip"][:, 0:2], M["recip"][:, 0:2],
               chain=True, inc="sda")
        for ti in range(len(tiles)):
            V.wait("swv", hidx * 2 + (ti + 1 if len(tiles) == 2 else 2))
            if h == 0:
                V.waitp(rec[("epicopy", st - 1, ti)])
            ob = M["out_sb"][ti]
            rc = M["recip"][:, ti:ti + 1]
            d2 = 2 if (len(tiles) == 1) else 1
            if h == 0:
                V.call("tensor_scalar_mul", ob[:, 0:512], banks[6][:], rc,
                       chain=True, inc=None)
                V.call("tensor_scalar_mul", ob[:, 512:1024], banks[7][:], rc,
                       chain=True, inc="sev", dinc=d2)
            else:
                V.call("scalar_tensor_tensor", ob[:, 0:512], banks[6][:], rc,
                       ob[:, 0:512], ALU.mult, ALU.add, chain=True, inc=None)
                V.call("scalar_tensor_tensor", ob[:, 512:1024], banks[7][:], rc,
                       ob[:, 512:1024], ALU.mult, ALU.add, chain=True, inc="sev",
                       dinc=d2)

    epilogue(NST - 1)
    A.waitp(A.prev)
    A.call("nop", inc="sph")
    A.wait("sph", 10)
    V.waitp(V.prev)
    V.call("nop", inc="sph")
    V.wait("sph", 10)


    def p3a_evac(ti):
        for par in range(2):
            V.waitp(rec[("utr", ti, 6 + par)])
            V.call("tensor_copy",
                   M["uTall"][:].rearrange("p (j2 two q) -> p j2 two q",
                                           two=2, q=NTILE * 128)[:, :, par,
                                                                 ti * 128:(ti + 1) * 128],
                   M["banks"][par][:].bitcast(BF16)[:, 0:512].rearrange(
                       "p (c b) -> p c b", b=128),
                   chain=True, inc=None)
            rec[("ueg", ti, par)] = V.prev
        if ti == 0:
            for j in range(8):
                V.call("tensor_scalar_mul", M["uT"][j][:, 127:128],
                       M["uT"][j][:, 127:128], M["mask_sb"][:, 0:1],
                       chain=True, inc=None)

    for i in range(NTILE + 6):
        if i < NTILE:
            V.wait(f"lxm{(i + 4) % 6}", 16 * (i // 6 + 1))
            ln_s1(V, A, M, rec, ("ua", i), M["xmb"][(i + 4) % 6][:], i % 4)
        if 0 <= i - 1 < NTILE:
            ln_s2(V, A, M, rec, ("ua", i - 1), (i - 1) % 4)
        if 0 <= i - 2 < NTILE:
            ln_s3(V, A, M, rec, ("ua", i - 2), (i - 2) % 4)
        if 0 <= i - 3 < NTILE:
            j = i - 3
            if j >= 4:
                A.waitp(rec[("utr", j - 4, 7)])
            ln_s4(V, A, M, rec, ("ua", j), M["xmb"][(j + 4) % 6][:],
                  M["ub"][j % 4][:], j % 4)
            rec[("ucopy", j)] = rec[("lncp", "ua", j)]
        if 0 <= i - 6 < NTILE:
            p3a_evac(i - 6)
    rec["p3a_v_done"] = V.prev

    def mixes(g):
        V.waitp(rec[("m2", g - 2, 1)])
        V.waitp(rec[("m1", g - 2, 7)])
        for j in range(8):
            cur = 128 + g * 256
            V.call("tensor_sub", M["d1"][:], M["uT"][j][:, cur:cur + 256],
                   M["uT"][j][:, cur - 1:cur + 255], chain=True, inc=None)
            V.call("scalar_tensor_tensor", M["xrT"][g % 2][:, j * 256:(j + 1) * 256],
                   M["d1"][:], M["mur_sb"][:, j:j + 1],
                   M["uT"][j][:, cur - 1:cur + 255], ALU.mult, ALU.add,
                   chain=True, inc=None)
            V.call("scalar_tensor_tensor", M["xkT"][g % 2][:, j * 256:(j + 1) * 256],
                   M["d1"][:], M["muk_sb"][:, j:j + 1],
                   M["uT"][j][:, cur - 1:cur + 255], ALU.mult, ALU.add,
                   chain=True, inc=None)
        rec[("mix", g)] = V.prev

    mixes(0)
    mixes(1)
    NG = 16
    for g in range(NG + 1):
        ge = g - 1
        if g < NG:
            for ti in range(2):
                A.waitp(rec[("m2", g, ti)])
                A.waitp(rec[("o2", g - 1, ti)])
                A.call("activation", M["rgate"][ti][:, 0:512], banks[4 + 2 * ti][:],
                       AF.Tanh, scale=M["consts"][:, 3:4], chain=True, inc=None)
                A.call("activation", M["rgate"][ti][:, 512:1024], banks[5 + 2 * ti][:],
                       AF.Tanh, scale=M["consts"][:, 3:4], chain=True, inc=None)
                rec[("sigexp", g, ti)] = A.prev
        if g < NG:
            for ti in range(2):
                V.waitp(rec[("sigexp", g, ti)])
                V.call("tensor_scalar_add", M["rgate"][ti][:], M["rgate"][ti][:],
                       1.0, chain=True, inc=None)
                rec[("rgate", g, ti)] = V.prev
            for j in range(8):
                A.waitp(rec[("m1", g, j + 4 if j < 4 else j)])
                if j == 0:
                    A.waitp(rec[("kact", g - 1)])
                A.call("activation", M["krelu"][:, j * 256:(j + 1) * 256],
                       banks[j % 4][:, (j // 4) * 256:(j // 4) * 256 + 256],
                       AF.Relu, chain=True, inc=None)
                rec[("relu1", g, j)] = A.prev
            rec[("relu", g)] = A.prev
            for j in range(8):
                V.waitp(rec[("relu1", g, j)])
                if j == 0:
                    V.waitp(rec[("m3", g - 1, 1)])
                V.call("tensor_mul", M["kactT"][:, j * 256:(j + 1) * 256],
                       M["krelu"][:, j * 256:(j + 1) * 256],
                       M["krelu"][:, j * 256:(j + 1) * 256], chain=True, inc=None)
            rec[("kact", g)] = V.prev
        if ge >= 0:
            for ti in range(2):
                st6 = M["stats6"][:, ti * 12:ti * 12 + 12]
                src = M["o2"][ti][:]
                V.call("bn_stats", st6[:, 0:6], src[:, 0:512], chain=True, inc=None)
                V.call("bn_stats", st6[:, 6:12], src[:, 512:1024],
                       chain=True, inc=None)
                if ge == 0:
                    V.waitp(rec[("lncp", "ua", 32 if ti == 0 else 29)])
                V.waitp(rec[("eplnA", ge - 1, ti)])
                V.call("bn_aggr", M["stats2"][:, ti * 2:ti * 2 + 2], st6,
                       chain=True, inc=None)
                rec[("epagg", ge, ti)] = V.prev
            for ti in range(2):
                A.waitp(rec[("epagg", ge, ti)])
                if ge == 0:
                    A.waitp(rec[("lnnm", "ua", 32 if ti == 0 else 29)])
                A.waitp(rec[("epnmr", ge - 1, ti)])
                A.call("activation", M["rstd"][:, ti:ti + 1],
                       M["stats2"][:, ti * 2 + 1:ti * 2 + 2], AF.Ln,
                       bias=M["consts"][:, 0:1], chain=True, inc=None)
                A.call("activation", M["rstd"][:, ti:ti + 1],
                       M["rstd"][:, ti:ti + 1], AF.Exp,
                       scale=M["consts"][:, 1:2], chain=True, inc=None)
                rec[("eplnA", ge, ti)] = A.prev
            for ti in range(2):
                V.waitp(rec[("eplnA", ge, ti)])
                V.waitp(rec[("ln3copy", ge - 1, ti)])
                V.call("tensor_scalar", M["nmr"][:, ti:ti + 1],
                       M["stats2"][:, ti * 2:ti * 2 + 1], -1.0,
                       M["rstd"][:, ti:ti + 1], ALU.mult, ALU.mult,
                       chain=True, inc=None)
                rec[("epnmr", ge, ti)] = V.prev
        if g < NG:
            for ti in range(2):
                V.waitp(rec[("m3", g, ti)])
                V.waitp(rec[("rgate", g, ti)])
                V.waitp(rec[("ln3copy", g - 1, ti)])
                V.call("scalar_tensor_tensor", M["o2"][ti][:, 0:512],
                       banks[4 + 2 * ti][:], 0.5, M["rgate"][ti][:, 0:512],
                       ALU.mult, ALU.mult, chain=True, inc=None)
                V.call("scalar_tensor_tensor", M["o2"][ti][:, 512:1024],
                       banks[5 + 2 * ti][:], 0.5, M["rgate"][ti][:, 512:1024],
                       ALU.mult, ALU.mult, chain=True, inc=None)
                rec[("o2", g, ti)] = V.prev
        if ge >= 0:
            for ti in range(2):
                A.waitp(rec[("epnmr", ge, ti)])
                A.wait("sst", 16 * (2 * (ge - 1) + ti + 1))
                A.call("activation", M["ystage"][ti][:], M["o2"][ti][:],
                       AF.Identity, bias=M["nmr"][:, ti:ti + 1],
                       scale=M["rstd"][:, ti:ti + 1], chain=True, inc=None)
                rec[("ln3copy", ge, ti)] = A.prev
                nld = NTILE + 2 * ge + ti
                V.wait(f"lxm{(nld + 4) % 6}", 16 * (nld // 6 + 1))
                V.waitp(rec[("ln3copy", ge, ti)])
                V.call("tensor_add", M["ystage"][ti][:], M["ystage"][ti][:],
                       M["xmb"][(nld + 4) % 6][:], chain=True, inc=None)
                rec[("yready", 2 * ge + ti)] = V.prev
        if g < NG and g + 2 < NG:
            mixes(g + 2)


def emit_dma(SY, G, M, rec):
    def xf_issue(idx):
        src = (M["xb_d"][idx * 128:(idx + 1) * 128, :] if idx < NK
               else M["xq_d"][(idx - NK) * 128:(idx - NK + 1) * 128, :])
        s = (idx + 4) % 6
        SY.call("dma_start", out=M["xf"][s][:], in_=src,
                inc=f"lxf{s}", dinc=16)

    for dst, src in M["small_loads"]:
        SY.call("dma_start", out=dst, in_=src, inc="lsm", dinc=16)
    for idx in range(2):
        xf_issue(idx)
    for job in range(len(M["wjobs"])):
        if job >= 4:
            SY.waitp(rec[("wconv", job - 4)])
        SY.call("dma_start", out=M["wldf"][job % 4][:], in_=M["wjob_src"][job],
                inc=f"lw{job % 4}", dinc=16)
    for idx in range(2, NK + NTILE):
        if idx < 6:
            SY.waitp(rec[("wconv", len(M["wjobs"]) - 1)])
        else:
            SY.waitp(rec[("lncopy", idx - 6)])
        xf_issue(idx)

    for w in M["wmat_writes"]:
        G.waitp(rec[("weg", w["last_job"], 1)])
        G.call("dma_start", out=w["dst"], in_=M["wstage"][:], inc="skw", dinc=16)
        G.wait("skw", G.counts["skw"])
    for gi in range(8):
        G.waitp(rec[("p1stage", gi * 8 + 7)])
        for p in range(4):
            G.call("dma_start",
                   out=M["kt_s"][p][:, gi * 1024:(gi + 1) * 1024],
                   in_=M["ktstage"][:].rearrange("p (a f) -> p a f", a=4)[:, p],
                   inc="skw", dinc=16)
        G.wait("skw", G.counts["skw"])
    for qg in range(17):
        qts = [2 * qg, 2 * qg + 1] if qg < 16 else [32]
        G.waitp(rec[("p1stage", NK + qts[-1])])
        w = len(qts) * 128
        for p in range(4):
            G.call("dma_start",
                   out=M["qt_s"][p][:, qts[0] * 128:qts[0] * 128 + w],
                   in_=M["qtstage"][:].rearrange("p (a f) -> p a f", a=4)[:, p][:, 0:w],
                   inc="skw", dinc=16)
        G.wait("skw", G.counts["skw"])

    SY.call("nop", inc="sph")
    SY.wait("sph", 5)
    G.wait("skw", 16 * (M["NKW0"] + 32 + 68))
    G.call("nop", inc="sph")
    G.wait("sph", 5)

    def load_qt(st):
        SY.waitp(rec[("qtfree", st - 2)])
        for h in range(NHEAD):
            rb = 64 * (h % 2)
            for half in range(2):
                SY.call("dma_start",
                        out=M["qt_buf"][st % 2][64 * half:64 * half + 64,
                                               h * 256:h * 256 + st_qw(st)],
                        in_=M["qt_s"][h // 2][rb:rb + 64,
                                              st * 256:st * 256 + st_qw(st)],
                        inc=f"sqt{st % 2}", dinc=16)

    def load_kt(ghq):
        if ghq >= NHIDX * 4:
            return
        SY.waitp(rec[("ktfree", ghq - 2)])
        hh = (ghq // 4) % NHEAD
        qq = ghq % 4
        rb = 64 * (hh % 2)
        srcv = M["kt_s"][hh // 2][rb:rb + 64, :].rearrange(
            "r (n two b) -> r n two b", two=2, b=128)
        dst = M["kt_buf"][ghq % 2]
        for par in range(2):
            SY.call("dma_start",
                    out=dst[64 * par:64 * par + 64, :].rearrange(
                        "r (c b) -> r c b", b=128),
                    in_=srcv[:, 8 * qq:8 * qq + 8, par, :],
                    inc=f"skt{ghq % 2}", dinc=16)

    load_qt(0)
    for q0 in range(4):
        load_kt(q0)
    for hidx in range(NHIDX):
        st, h = hidx // NHEAD, hidx % NHEAD
        if h == 4 and st + 1 < NST:
            load_qt(st + 1)
        if hidx >= 1:
            SY.wait("swv", hidx * 2)
        for half2 in range(2):
            SY.call("dma_start", out=M["wvt_buf"][half2][:],
                    in_=M["wvt_s"][h][:, half2 * 4096:(half2 + 1) * 4096],
                    inc=f"swt{half2}", dinc=16)
        for qq in range(4):
            load_kt((hidx + 1) * 4 + qq)
        if h == 5:
            for tile in st_tiles(st):
                if tile >= 2:
                    SY.wait("sxm", 16 * (tile - 1))
                SY.call("dma_start", out=M["xq_buf"][tile % 2][:],
                        in_=M["xq_d"][tile * 128:(tile + 1) * 128, :],
                        inc=f"sxq{tile % 2}", dinc=16)
    for tile in range(NTILE):
        G.waitp(rec[("xmid", tile)])
        G.call("dma_start", out=M["xm_s"][tile], in_=M["xq_buf"][tile % 2][:],
               inc="sxm", dinc=16)
        G.wait("sxm", G.counts["sxm"])

    SY.call("nop", inc="sph")
    SY.wait("sph", 10)
    G.wait("sxm", 16 * NTILE)
    G.call("nop", inc="sph")
    G.wait("sph", 10)

    for ti in range(6):
        SY.call("dma_start", out=M["xmb"][(ti + 4) % 6][:], in_=M["xm_s"][ti],
                inc=f"lxm{(ti + 4) % 6}", dinc=16)
    for i, wsb in enumerate((M["wrT_sb"], M["wkT_sb"], M["wvT_sb"])):
        SY.call("dma_start", out=wsb[:], in_=M["wft_s"][i], inc="lfw", dinc=16)
    for ti in range(6, NTILE):
        SY.waitp(rec[("ucopy", ti - 6)])
        SY.call("dma_start", out=M["xmb"][(ti + 4) % 6][:], in_=M["xm_s"][ti],
                inc=f"lxm{(ti + 4) % 6}", dinc=16)
    for g in range(16):
        for ti in range(2):
            tile = 2 * g + 1 + ti
            nld = NTILE + 2 * g + ti
            prev = nld - 6
            if prev < NTILE:
                SY.waitp(rec[("ucopy", prev)])
            else:
                SY.waitp(rec[("yready", prev - NTILE)])
            SY.call("dma_start", out=M["xmb"][(nld + 4) % 6][:],
                    in_=M["xm_s"][tile], inc=f"lxm{(nld + 4) % 6}", dinc=16)
    for g in range(16):
        for ti in range(2):
            G.waitp(rec[("yready", 2 * g + ti)])
            G.call("dma_start",
                   out=M["y_d"][(2 * g + ti) * 128:(2 * g + ti + 1) * 128, :],
                   in_=M["ystage"][ti][:], inc="sst", dinc=16)
            G.wait("sst", G.counts["sst"])
    G.wait("sst", 16 * 32)



def build():
    nc = bass.Bass("TRN2", target_bir_lowering=False)

    M = {}
    M["xb_d"] = nc.dram_tensor("xb", [NK * 128, 1024], F32, kind="ExternalInput").ap()
    M["xq_d"] = nc.dram_tensor("xq", [NTILE * 128, 1024], F32, kind="ExternalInput").ap()
    wq_d = nc.dram_tensor("wq", [512, 1024], F32, kind="ExternalInput").ap()
    wk_d = nc.dram_tensor("wk", [512, 1024], F32, kind="ExternalInput").ap()
    wv_d = nc.dram_tensor("wv", [8192, 1024], F32, kind="ExternalInput").ap()
    wr_d = nc.dram_tensor("wr", [1024, 1024], F32, kind="ExternalInput").ap()
    wkf_d = nc.dram_tensor("wkf", [1024, 1024], F32, kind="ExternalInput").ap()
    wvf_d = nc.dram_tensor("wvf", [1024, 1024], F32, kind="ExternalInput").ap()
    mur_d = nc.dram_tensor("mur", [128, 8], F32, kind="ExternalInput").ap()
    muk_d = nc.dram_tensor("muk", [128, 8], F32, kind="ExternalInput").ap()
    mask_d = nc.dram_tensor("mask", [128, 1], F32, kind="ExternalInput").ap()
    M["y_d"] = nc.dram_tensor("y", [32 * 128, 1024], F32, kind="ExternalOutput").ap()

    M["kt_s"] = nc.dram_tensor("kt_s", [4, 128, 8192], BF16).ap()
    M["qt_s"] = nc.dram_tensor("qt_s", [4, 128, NTILE * 128], BF16).ap()
    M["wvt_s"] = nc.dram_tensor("wvt_s", [8, 128, 8192], BF16).ap()
    M["wft_s"] = nc.dram_tensor("wft_s", [3, 128, 8192], BF16).ap()
    M["xm_s"] = nc.dram_tensor("xm_s", [NTILE, 128, 1024], F32).ap()

    ident_bf_d = nc.inline_tensor(np.eye(128, dtype=BFNP), name="identbf").ap()
    ident_f_d = nc.inline_tensor(np.eye(128, dtype=np.float32), name="identf_d").ap()
    cst = np.zeros((128, 8), np.float32)
    cst[:, 0] = 1e-5
    cst[:, 1] = -0.5
    cst[:, 2] = -1.0
    cst[:, 3] = 0.5
    consts_d = nc.inline_tensor(cst, name="consts_d").ap()

    BASE = 16512

    def at(name, shape, dtype, off):
        return nc.alloc_sbuf_tensor_at(name, shape, dtype, offset=off)

    M["x_ln"] = at("x_ln", [128, NK * 1024], BF16, BASE)
    M["uTall"] = at("uTall", [128, 8 * NTILE * 128], BF16, BASE)
    M["uT"] = [M["uTall"][:, j * NTILE * 128:(j + 1) * NTILE * 128]
               for j in range(8)]
    M["wrT_sb"] = at("wrT", [128, 8192], BF16, BASE + 67584)
    M["wkT_sb"] = at("wkfT", [128, 8192], BF16, BASE + 83968)
    M["wvT_sb"] = at("wvfT", [128, 8192], BF16, BASE + 100352)

    B0 = 147584
    M["consts"] = at("consts", [128, 8], F32, B0)
    M["stats6"] = at("stats6", [128, 48], F32, B0 + 32)
    M["stats2"] = at("stats2", [128, 8], F32, B0 + 224)
    M["nmr"] = at("nmr", [128, 4], F32, B0 + 256)
    M["rstd"] = at("rstd", [128, 4], F32, B0 + 288)
    M["mask_sb"] = at("mask_sb", [128, 1], F32, B0 + 320)
    M["mur_sb"] = at("mur_sb", [128, 8], F32, B0 + 352)
    M["muk_sb"] = at("muk_sb", [128, 8], F32, B0 + 384)
    M["identb"] = at("identb", [128, 128], BF16, B0 + 416)
    M["identf"] = at("identf", [128, 128], F32, B0 + 672)
    BP = B0 + 1184
    TOPB = 229344

    o = BP
    M["wqT_p1"] = at("wqT_p1", [128, 4096], BF16, o); o += 8192
    M["wkT_p1"] = at("wkT_p1", [128, 4096], BF16, o); o += 8192
    OV = o
    M["wldf"] = [at(f"wldf{i}", [128, 1024], F32, OV + i * 4096) for i in range(4)]
    M["wldb"] = [at(f"wldb{i}", [128, 1024], BF16, OV + 16384 + i * 2048)
                 for i in range(4)]
    M["wstage"] = at("wstage", [128, 8192], BF16, OV + 24576)
    M["xf"] = [at(f"xf{i}", [128, 1024], F32, OV + i * 4096) for i in range(4)]
    M["xf"] += [at(f"xf{i}", [128, 1024], F32, BP + 65536 + (i - 4) * 4096)
                for i in range(4, 6)]
    M["xlt"] = [at(f"xlt{i}", [128, 1024], BF16, OV + 16384 + i * 2048)
                for i in range(4)]
    M["ktstage"] = at("ktstage", [128, 4096], BF16, OV + 24576)
    M["qtstage"] = at("qtstage", [128, 1024], BF16, OV + 32768)
    o = OV + 40960
    M["qlnb"] = [at(f"qlnb{i}", [128, 1024], BF16, o + i * 2048) for i in range(4)]
    o += 8192
    assert o <= TOPB, o

    o = BP
    M["kt_buf"] = [at(f"ktb{i}", [128, 1024], BF16, o + i * 2048) for i in range(2)]
    o += 4096
    M["wvt_buf"] = [at(f"wvtb{i}", [128, 4096], BF16, o + i * 8192) for i in range(2)]
    o += 16384
    M["qt_buf"] = [at(f"qtb{i}", [128, 2048], BF16, o + i * 4096) for i in range(2)]
    o += 8192
    M["pt"] = []
    for s in range(NPT):
        M["pt"].append(at(f"pt{s}", [128, 256], BF16, o)); o += 512
    M["xq_buf"] = [at(f"xqb{i}", [128, 1024], F32, o + i * 4096) for i in range(2)]
    o += 8192
    M["yt"] = []
    for c in range(8):
        M["yt"].append(at(f"yt{c}", [128, 256], BF16, o)); o += 512
    M["out_sb"] = [at(f"outsb{i}", [128, 1024], F32, o + i * 4096) for i in range(2)]
    o += 8192
    M["tmpf"] = at("tmpf", [128, 1024], F32, o); o += 4096
    M["dacc"] = at("dacc", [128, 256], F32, o); o += 1024
    M["recip"] = at("recip", [128, 2], F32, o); o += 32
    assert o <= TOPB, o

    o = BP
    M["xmb"] = [at(f"xmb{i}", [128, 1024], F32, o + i * 4096) for i in range(4)]
    M["xmb"] += [at(f"xmb{i}", [128, 1024], F32, BASE + 116736 + (i - 4) * 4096)
                 for i in range(4, 6)]
    o += 16384
    M["ub"] = [at(f"ub{i}", [128, 1024], BF16, o + i * 2048) for i in range(4)]
    o += 8192
    M["xrT"] = [at(f"xrT{i}", [128, 2048], BF16, o + i * 4096) for i in range(2)]
    o += 8192
    M["xkT"] = [at(f"xkT{i}", [128, 2048], BF16, o + i * 4096) for i in range(2)]
    o += 8192
    M["d1"] = at("d1", [128, 256], F32, o); o += 1024
    M["rgate"] = [at(f"rgate{i}", [128, 1024], F32, o + i * 4096) for i in range(2)]
    o += 8192
    M["krelu"] = at("krelu", [128, 2048], BF16, o); o += 4096
    M["kactT"] = at("kactT", [128, 2048], BF16, o); o += 4096
    M["o2"] = [at(f"o2_{i}", [128, 1024], F32, o + i * 4096) for i in range(2)]
    o += 8192
    M["ystage"] = [at(f"ystage{i}", [128, 1024], F32, o + i * 4096) for i in range(2)]
    o += 8192
    assert o <= TOPB, o

    banks = [nc.alloc_psum_tensor(f"pb{i}", [128, 512], F32) for i in range(8)]
    M["banks"] = banks
    M["bank_bf"] = None

    wjobs, wjob_src, wjob_dest = [], [], []
    wjob_prev_write, wmat_writes = [], []
    nw = 0

    def add_matrix(kind, src_d, ntiles, direct_dest=None, scratch_dst=None,
                   blkw=512):
        nonlocal nw
        first = len(wjobs)
        for t in range(ntiles):
            wjobs.append((kind, t))
            wjob_src.append(src_d[t * 128:(t + 1) * 128, :])
            dd = direct_dest if direct_dest is not None else M["wstage"]
            w = blkw if direct_dest is not None else 1024
            v = dd[:].rearrange("p (j2 two q) -> p j2 two q", two=2, q=w)
            wjob_dest.append([v[:, :, par, t * 128:(t + 1) * 128]
                              for par in range(2)])
            wjob_prev_write.append(nw if (t == 0 and scratch_dst is not None
                                          and nw > 0) else None)
        if scratch_dst is not None:
            nw += 1
            wmat_writes.append({"dst": scratch_dst,
                                "last_job": first + ntiles - 1})

    add_matrix("wq", wq_d, 4, direct_dest=M["wqT_p1"], blkw=512)
    add_matrix("wk", wk_d, 4, direct_dest=M["wkT_p1"], blkw=512)
    add_matrix("wr", wr_d, 8, scratch_dst=M["wft_s"][0])
    add_matrix("wkf", wkf_d, 8, scratch_dst=M["wft_s"][1])
    add_matrix("wvf", wvf_d, 8, scratch_dst=M["wft_s"][2])
    for h in range(8):
        add_matrix(f"wv{h}", wv_d[h * 1024:(h + 1) * 1024, :], 8,
                   scratch_dst=M["wvt_s"][h])
    M["wjobs"] = wjobs
    M["wjob_src"] = wjob_src
    M["wjob_destg"] = wjob_dest
    M["wjob_prev_write"] = wjob_prev_write
    M["wmat_writes"] = wmat_writes
    M["NKW0"] = nw
    M["small_loads"] = [
        (M["identb"][:], ident_bf_d),
        (M["identf"][:], ident_f_d),
        (M["consts"][:, 0:8], consts_d),
        (M["mur_sb"][:], mur_d),
        (M["muk_sb"][:], muk_d),
        (M["mask_sb"][:], mask_d),
    ]
    M["NLD0"] = 6 + len(wjobs)
    M["NLD1"] = M["NLD0"] + (NK + NTILE) + 3
    M["NLD2"] = M["NLD1"] + NTILE

    NJOB = len(wjobs)
    rec = Rec()
    for job in range(NJOB):
        for j in range(8):
            rec[("wtr", job, j)] = ("sto", job * 8 + j + 1)
    for j in range(8):
        rec[("p1tr", 0, j)] = ("sto", NJOB * 8 + j + 1)
    for idx in range(1, NK + NTILE):
        for j in range(8):
            rec[("p1tr", idx, j)] = ("sto", NJOB * 8 + 8 + (idx - 1) * 9 + j + 1)
    STO_P3 = NJOB * 8 + (NK + NTILE) * 9
    for ti in range(NTILE):
        for j in range(8):
            rec[("utr", ti, j)] = ("sto", STO_P3 + ti * 8 + j + 1)

    _LN_LAST.clear()
    counts = {n: 0 for n in SEMS}
    emit_tensor(Eng("t", counts, "sto"), M, rec)
    A0 = Eng("a", counts, "safo")
    V0 = Eng("v", counts, "svo")
    emit_scalar_vector(A0, V0, M, rec)
    emit_dma(Eng("s", counts, "ld"), Eng("g", counts, "skw"), M, rec)
    totals = dict(counts)

    _LN_LAST.clear()
    counts2 = {n: 0 for n in SEMS}
    Tr = Eng("t", counts2, "sto", record=True)
    emit_tensor(Tr, M, rec)
    Ar = Eng("a", counts2, "safo", record=True)
    Vr = Eng("v", counts2, "svo", record=True)
    emit_scalar_vector(Ar, Vr, M, rec)
    Sr = Eng("s", counts2, "ld", record=True)
    Gr = Eng("g", counts2, "skw", record=True)
    emit_dma(Sr, Gr, M, rec)
    assert counts2 == totals, (counts2, totals)

    from contextlib import ExitStack
    ctx = ExitStack()
    sems = {n: ctx.enter_context(nc.semaphore(n)) for n in SEMS}

    with nc.Block() as block:
        @block.tensor
        def _(e):
            replay(e, sems, Tr.ops)

        @block.scalar
        def _(e):
            replay(e, sems, Ar.ops)

        @block.vector
        def _(e):
            replay(e, sems, Vr.ops)

        @block.sync
        def _(e):
            replay(e, sems, Sr.ops)

        @block.gpsimd
        def _(e):
            replay(e, sems, Gr.ops)

    return nc



_NC_CACHE = {}


def kernel(L0_attn_x, L0_attn_Wq, L0_attn_Wk, L0_attn_Wv,
           L0_ffn_W_r, L0_ffn_W_k, L0_ffn_W_v, L0_ffn_mu_r, L0_ffn_mu_k):
    x = np.ascontiguousarray(np.asarray(L0_attn_x, dtype=np.float32))
    wq = np.ascontiguousarray(np.asarray(L0_attn_Wq, dtype=np.float32))
    wk = np.ascontiguousarray(np.asarray(L0_attn_Wk, dtype=np.float32))
    wv = np.ascontiguousarray(np.asarray(L0_attn_Wv, dtype=np.float32))
    wr = np.ascontiguousarray(np.asarray(L0_ffn_W_r, dtype=np.float32))
    wkf = np.ascontiguousarray(np.asarray(L0_ffn_W_k, dtype=np.float32))
    wvf = np.ascontiguousarray(np.asarray(L0_ffn_W_v, dtype=np.float32))
    mur = np.asarray(L0_ffn_mu_r, dtype=np.float32).reshape(8, 128).T.copy()
    muk = np.asarray(L0_ffn_mu_k, dtype=np.float32).reshape(8, 128).T.copy()

    if "nc" not in _NC_CACHE:
        _NC_CACHE["nc"] = build()
    nc = _NC_CACHE["nc"]

    in_maps = []
    for core in range(8):
        b, half = core // 2, core % 2
        xb = x[b]
        if half == 0:
            xq = np.concatenate([xb[0:128], xb[0:4096]], axis=0)
            mask = np.zeros((128, 1), np.float32)
        else:
            xq = xb[3968:8192]
            mask = np.ones((128, 1), np.float32)
        in_maps.append(dict(xb=xb, xq=np.ascontiguousarray(xq), wq=wq, wk=wk,
                            wv=wv, wr=wr, wkf=wkf, wvf=wvf, mur=mur, muk=muk,
                            mask=mask))
    res = run_bass_kernel_spmd(nc, in_maps, core_ids=list(range(8)))
    y = np.empty((4, 8192, 1024), np.float32)
    for core in range(8):
        b, half = core // 2, core % 2
        y[b, half * 4096:(half + 1) * 4096, :] = res.results[core]["y"]
    return y
